# version 04
# baseline (speedup 1.0000x reference)
import numpy as np
import concourse.bass as bass
import concourse.mybir as mybir

F32 = mybir.dt.float32
F32R = mybir.dt.float32r
BF16 = mybir.dt.bfloat16
I32 = mybir.dt.int32
AF = mybir.ActivationFunctionType
ALU = mybir.AluOpType
AX = mybir.AxisListType

L = 4096
D = 1024
NLC = 32
NDT = 8
DTR = 64
NS = 16

ENGS = ("tensor", "vector", "scalar", "gpsimd", "sync")
C2_HEADS = 16
C2_HEADBAR = False
DBG_DUMPS = False


class Tok:
    __slots__ = ("eng", "idx", "dma")
    def __init__(self, eng, idx, dma):
        self.eng = eng; self.idx = idx; self.dma = dma


class Prog:
    def __init__(self, nc):
        self.nc = nc
        self.records = {e: [] for e in ENGS}
        self.sems = {}

    def op(self, eng, fn, waits=(), dma=False, fence=False):
        ws = [w for w in waits if w is not None and (fence or w.dma or w.eng != eng)]
        for w in ws:
            self.records[w.eng][w.idx][3][0] = True
        self.records[eng].append([fn, ws, dma, [bool(dma)]])
        return Tok(eng, len(self.records[eng]) - 1, dma)

    def barrier(self, toks):
        for e in ENGS:
            self.op(e, None, waits=toks)

    DPOOL = 24

    def finalize(self, block):
        for e in ENGS:
            self.sems[e] = self.nc.alloc_semaphore(f"es_{e}")
        dpool = {}
        for e in ("sync", "gpsimd"):
            dpool[e] = [self.nc.alloc_semaphore(f"dp_{e}_{i}") for i in range(self.DPOOL)]
        rank = {e: [] for e in ENGS}
        dinfo = {e: [] for e in ENGS}
        for e in ENGS:
            r = 0; dcount = 0
            for fn, ws, dma, marked in self.records[e]:
                if dma:
                    pi = dcount % self.DPOOL
                    val = 16 * (dcount // self.DPOOL + 1)
                    dinfo[e].append((pi, val))
                    dcount += 1
                else:
                    dinfo[e].append(None)
                    if marked[0]:
                        r += 1
                rank[e].append(r)
        sems = self.sems

        def section(e):
            def body(eng):
                for i, (fn, ws, dma, marked) in enumerate(self.records[e]):
                    for w in ws:
                        if w.dma:
                            pi, val = dinfo[w.eng][w.idx]
                            eng.wait_ge(dpool[w.eng][pi], val)
                        else:
                            eng.wait_ge(sems[w.eng], rank[w.eng][w.idx])
                    if fn is None:
                        continue
                    if dma:
                        pi, val = dinfo[e][i]
                        if val > 16:
                            eng.wait_ge(dpool[e][pi], val - 16)
                        fn(eng).then_inc(dpool[e][pi], 16)
                    else:
                        ins = fn(eng)
                        if marked[0]:
                            ins.then_inc(sems[e], 1)
            return body
        for e in ENGS:
            getattr(block, e)(section(e))


class Arena:
    def __init__(self, nc, f32_cols, bf16_cols=0):
        self.f32t = nc.alloc_sbuf_tensor("arena_f32", [128, f32_cols], F32).ap()
        self.f32_cols = f32_cols
        self.reset()

    def reset(self):
        self.f32_off = 0

    def afp(self, cols, parts=128):
        o = self.f32_off; self.f32_off += cols
        assert self.f32_off <= self.f32_cols, f"f32 arena overflow {self.f32_off}"
        return self.f32t[0:parts, o:o + cols]

    def abf(self, cols, parts=128):
        n = (cols + 1) // 2
        o = self.f32_off; self.f32_off += n
        assert self.f32_off <= self.f32_cols, f"f32 arena overflow {self.f32_off}"
        return self.f32t[:, o:o + n].bitcast(BF16)[0:parts, 0:cols]


def host_constants():
    triu = np.triu(np.ones((128, 128), dtype=np.float32))
    iden = np.eye(128, dtype=np.float32)
    ones_row = np.ones((1, 128), dtype=np.float32)
    ones_col = np.ones((128, 1), dtype=np.float32)
    indic = np.zeros((128, 16 * 16), dtype=np.float32)
    for n in range(16):
        indic[:, 16 * n + n] = 1.0
    return {"c_triu": triu, "c_iden": iden, "c_ones_row": ones_row,
            "c_ones_col": ones_col, "c_indic": indic}


class Ctx:
    def __init__(self, nc, P, ar):
        self.nc, self.P, self.ar = nc, P, ar
        alloc = nc.alloc_sbuf_tensor
        self.triu = alloc("triu", [128, 128], F32).ap()
        self.iden = alloc("iden", [128, 128], F32).ap()
        self.idenb = alloc("idenb", [128, 128], BF16).ap()
        self.onesr = alloc("onesr", [1, 128], F32).ap()
        self.onesc = alloc("onesc", [128, 1], F32).ap()
        self.indic = alloc("indic", [128, 256], F32).ap()
        self.int_t = alloc("int_t", [128, 1024], I32).ap()
        self.ccol = alloc("ccol", [128, 4], F32).ap()
        self.pb = [nc.alloc_psum_tensor(f"pb{i}", [128, 1024], F32).ap()
                   for i in range(4)]

    def pbf(self, i, half):
        return self.pb[i][:, half * 512:half * 512 + 64].bitcast(BF16)


def emit_const_loads(C, dins):
    P = C.P
    t = None
    for dst, src in [(C.triu, dins["c_triu"]), (C.iden, dins["c_iden"]),
                     (C.onesr, dins["c_ones_row"]), (C.onesc, dins["c_ones_col"]),
                     (C.indic, dins["c_indic"])]:
        t = P.op("sync", lambda e, d=dst, s=src: e.dma_start(d, s), dma=True)
    tb = P.op("vector", lambda e: e.tensor_copy(C.idenb, C.iden), waits=[t])
    P.op("vector", lambda e: e.memset(C.ccol[:, 0:1], 1e-5))
    P.op("vector", lambda e: e.memset(C.ccol[:, 1:2], 1e-12))
    P.op("vector", lambda e: e.memset(C.ccol[:, 2:3], 1e-6))
    tb = P.op("vector", lambda e: e.memset(C.ccol[:, 3:4], 1e-8))
    return tb


def emit_quantize(C, src, dst_bf, ws, waits):
    P = C.P
    V = lambda fn, w=(): P.op("vector", fn, waits=w)
    S = lambda fn, w=(): P.op("scalar", fn, waits=w)
    wa, wb, wc, wd = ws["wa"], ws["wb"], ws["wc"], ws["wd"]
    sc, inv = ws["sc"], ws["inv"]
    v = lambda ap, w: ap.rearrange("p (r c) -> p r c", c=w)
    t6 = V(lambda e: e.reduce_max(wb[:, 0:32].unsqueeze(2), v(src, 32), axis=AX.X,
                                  apply_absolute_value=True), waits)
    t7 = P.op("vector", lambda e: e.tensor_scalar(sc, wb[:, 0:32], 1.0 / 7.5, 1e-30,
                                                  ALU.mult, ALU.max), waits=[t6], fence=True)
    t8a = S(lambda e: e.activation(inv, sc, AF.Ln), [t7])
    t8 = P.op("scalar", lambda e: e.activation(inv, inv, AF.Exp, scale=-1.0),
              waits=[t8a], fence=True)
    bc = lambda ap: ap.unsqueeze(2).broadcast_to([128, 32, 32])
    t9 = V(lambda e: e.tensor_tensor(v(wa, 32), v(src, 32), bc(inv), ALU.mult), [t8])
    ta = V(lambda e: e.tensor_scalar(wb, wa, -1.0, None, ALU.mult), [t9])
    tb = V(lambda e: e.tensor_tensor(wb, wa, wb, ALU.max), [ta])
    tc = V(lambda e: e.tensor_scalar(wc, wb, 2.0, None, ALU.is_ge), [tb])
    td = V(lambda e: e.tensor_scalar(wd, wb, 4.0, None, ALU.is_ge), [tc])
    te = V(lambda e: e.tensor_scalar(wb, wc, -4.0, 8.0, ALU.mult, ALU.add), [td])
    tf = V(lambda e: e.scalar_tensor_tensor(wb, wd, -2.0, wb, ALU.mult, ALU.add), [te])
    tg = V(lambda e: e.tensor_scalar(wc, wc, 0.125, 0.125, ALU.mult, ALU.add), [tf])
    th = V(lambda e: e.scalar_tensor_tensor(wc, wd, 0.25, wc, ALU.mult, ALU.add), [tg])
    ti = V(lambda e: e.tensor_tensor(wa, wa, wb, ALU.mult), [th])
    tj = V(lambda e: e.tensor_copy(C.int_t, wa), [ti])
    tk = V(lambda e: e.tensor_copy(wa, C.int_t), [tj])
    tl = V(lambda e: e.tensor_tensor(wa, wa, wc, ALU.mult), [tk])
    tm = V(lambda e: e.tensor_tensor(v(dst_bf, 32), v(wa, 32), bc(sc), ALU.mult), [tl])
    return tm


def emit_transpose8(C, src, dst_slices, waits, bf16=False, pb_idx=0):
    P = C.P
    last_v = {0: None, 1: None}
    tv = None
    for j in range(8):
        h = j % 2
        if bf16:
            pst = C.pbf(pb_idx, h)
        else:
            pst = C.pb[pb_idx][:, h * 512:h * 512 + 128]
        iden = C.idenb if bf16 else C.iden
        w = list(waits) + ([last_v[h]] if last_v[h] else [])
        tt = P.op("tensor", lambda e, j=j, pst=pst, iden=iden:
                  e.transpose(pst, src[:, j * 128:(j + 1) * 128], iden), waits=w)
        tv = P.op("vector", lambda e, j=j, pst=pst:
                  e.tensor_copy(dst_slices[j], pst), waits=[tt])
        last_v[h] = tv
    return tv


def std8(dst):
    return [dst[:, j * 128:(j + 1) * 128] for j in range(8)]


def emit_phaseA(C, din, scr, waits0):
    P, ar = C.P, C.ar
    V = lambda fn, w=(): P.op("vector", fn, waits=w)
    S = lambda fn, w=(): P.op("scalar", fn, waits=w)
    T = lambda fn, w=(): P.op("tensor", fn, waits=w)
    Y = lambda fn, w=(): P.op("sync", fn, waits=w, dma=True)
    G = lambda fn, w=(): P.op("gpsimd", fn, waits=w, dma=True)

    Wx = ar.afp(768)
    Wdt = ar.afp(1024, parts=64)
    bdt_rep = ar.afp(1024); fb_rep = ar.afp(1024)
    bdt_row = ar.afp(1024, parts=1); fb_row = ar.afp(1024, parts=1)
    pdcar = ar.afp(1024, parts=1)
    xnT = ar.afp(1024)
    xdbl_sb = ar.afp(3072)
    dT = ar.afp(128, parts=64)
    xt = [ar.afp(1024), ar.afp(1024)]
    xn_t = ar.afp(1024); xc = ar.afp(1024); delta_t = ar.afp(1024)
    wa, wb, wc, wd = ar.afp(1024), ar.afp(1024), ar.afp(1024), ar.afp(1024)
    w_soft = ar.afp(1024); w_du = ar.afp(1024); w_pd = ar.afp(1024)
    sc, inv = ar.afp(32), ar.afp(32)
    mcol = ar.afp(1); mcol2 = ar.afp(1); vcol = ar.afp(1); rstd = ar.afp(1)
    ws = {"wa": wa, "wb": wb, "wc": wc, "wd": wd, "sc": sc, "inv": inv}
    xd_bf = ar.abf(1024); xdT_bf = ar.abf(1024); wdq_bf = ar.abf(1024)
    wdT_bf = ar.abf(8192)
    wsrc = ar.afp(1024)

    tWx = Y(lambda e: e.dma_start(Wx.rearrange("p (t e) -> p t e", e=96),
                                  din["L0_attn_W_x"].rearrange("(t p) e -> p t e", p=128)), waits0)
    tWdt = Y(lambda e: e.dma_start(Wdt, din["L0_attn_W_dt"]))
    tbdt = Y(lambda e: e.dma_start(bdt_row, din["L0_attn_b_dt"].rearrange("(o c) -> o c", o=1)))
    tfb = Y(lambda e: e.dma_start(fb_row, din["L0_ffn_bias"].rearrange("(o c) -> o c", o=1)))
    reps = []
    for (row, rep, tl) in [(bdt_row, bdt_rep, tbdt), (fb_row, fb_rep, tfb)]:
        tvh = None
        for h in range(2):
            tm = T(lambda e, row=row, h=h: e.matmul(
                C.pb[0][:, h * 512:(h + 1) * 512], C.onesr, row[:, h * 512:(h + 1) * 512],
                start=True, stop=True), [tl] + list(waits0) + ([tvh] if tvh else []))
            tvh = V(lambda e, rep=rep, h=h: e.tensor_copy(
                rep[:, h * 512:(h + 1) * 512], C.pb[0][:, h * 512:(h + 1) * 512]), [tm])
        reps.append(tvh)
    trep = reps[-1]
    tzero = V(lambda e: e.memset(pdcar, 0.0))

    ttr_w = trep
    tq_w = None
    t_dmp = None
    for ot in range(8):
        tl = Y(lambda e, ot=ot: e.dma_start(wsrc, din["L0_ffn_weight"][ot * 128:(ot + 1) * 128, :]),
               [tq_w] if tq_w else list(waits0))
        tq_w = emit_quantize(C, wsrc, wdq_bf, ws, [tl, ttr_w, t_dmp])
        if DBG_DUMPS:
            t_dmp = G(lambda e, ot=ot: e.dma_start(scr["wdq"][ot * 128:(ot + 1) * 128, :], wdq_bf), [tq_w])
        dsts = [wdT_bf[:, dt * 1024 + ot * 128: dt * 1024 + (ot + 1) * 128] for dt in range(8)]
        ttr_w = emit_transpose8(C, wdq_bf, dsts, [tq_w], bf16=True, pb_idx=0)

    tails = {}
    t_xcon = [None, None]
    t_pb1_free = ttr_w
    t_pb2a_free = None
    t_pb2b_free = None
    t_pdmm = None
    t_colsum = None
    t_g0dma = None
    t_pddma = None
    t_dudma = None
    t_xndma = None
    t_tr = ttr_w

    for lc in range(NLC):
        r0, r1 = lc * 128, (lc + 1) * 128
        bufx = xt[lc % 2]
        tldx = Y(lambda e, bufx=bufx, r0=r0, r1=r1: e.dma_start(bufx, din["x"][r0:r1, :]),
                 [t_xcon[lc % 2]])
        t1 = V(lambda e, bufx=bufx: e.reduce_sum(mcol, bufx, axis=AX.X), [tldx])
        tms = S(lambda e: e.mul(mcol2, mcol, 1.0 / 1024.0), [t1])
        t_xc = V(lambda e, bufx=bufx: e.tensor_scalar(xc, bufx, mcol2, None, ALU.subtract), [tms])
        t_xcon[lc % 2] = t_xc
        V(lambda e: e.tensor_tensor(wa, xc, xc, ALU.mult), [t_g0dma])
        tvs = V(lambda e: e.reduce_sum(vcol, wa, axis=AX.X))
        tl0 = S(lambda e: e.activation(rstd, vcol, AF.Ln, bias=C.ccol[:, 0:1], scale=1.0 / 1024.0), [tvs])
        tst = P.op("scalar", lambda e: e.activation(rstd, rstd, AF.Exp, scale=-0.5),
                   waits=[tl0], fence=True)
        t_xn = V(lambda e: e.tensor_scalar(xn_t, xc, rstd, None, ALU.mult), [tst, t_xndma])
        t_xndma = Y(lambda e, r0=r0, r1=r1: e.dma_start(scr["xn"][r0:r1, :], xn_t), [t_xn])
        t_tr = emit_transpose8(C, xn_t, std8(xnT), [t_xn, t_tr], bf16=False, pb_idx=0)
        tm = None
        for j in range(8):
            tm = T(lambda e, j=j: e.matmul(C.pb[2][:, 0:96], xnT[:, j * 128:(j + 1) * 128],
                                           Wx[:, j * 96:(j + 1) * 96],
                                           start=(j == 0), stop=(j == 7)),
                   [t_tr, tWx, t_pb2a_free] if j == 0 else [])
        t_xdbl = V(lambda e, lc=lc: e.tensor_copy(xdbl_sb[:, lc * 96:(lc + 1) * 96], C.pb[2][:, 0:96]), [tm])
        t_pb2a_free = t_xdbl
        Y(lambda e, lc=lc, r0=r0, r1=r1: e.dma_start(
            scr["xdbl"][r0:r1, :], xdbl_sb[:, lc * 96:(lc + 1) * 96]), [t_xdbl])
        ttr64 = T(lambda e, lc=lc: e.transpose(C.pb[2][0:64, 512:640],
                                               xdbl_sb[:, lc * 96:lc * 96 + 64], C.iden),
                  [t_xdbl, t_pb2b_free])
        tdT = V(lambda e: e.tensor_copy(dT, C.pb[2][0:64, 512:640]), [ttr64])
        t_pb2b_free = tdT
        tdm = None
        for h in range(2):
            tdm = T(lambda e, h=h: e.matmul(C.pb[1][:, h * 512:(h + 1) * 512], dT,
                                            Wdt[:, h * 512:(h + 1) * 512], start=True, stop=True),
                    [tdT, tWdt, t_pb1_free] if h == 0 else [])
        tz = V(lambda e: e.tensor_tensor(w_soft, C.pb[1], bdt_rep, ALU.add), [tdm, trep])
        t_pb1_free = tz
        tmx = V(lambda e: e.tensor_scalar(wa, w_soft, 0.0, None, ALU.max))
        tmn = V(lambda e: e.tensor_scalar(wb, w_soft, 0.0, None, ALU.min))
        S(lambda e: e.activation(wb, wb, AF.Exp), [tmn])
        te2 = S(lambda e: e.activation(wc, wa, AF.Exp, scale=-1.0), [tmx])
        tse = V(lambda e: e.tensor_tensor(wb, wb, wc, ALU.add), [te2])
        tl1 = S(lambda e: e.activation(wb, wb, AF.Ln), [tse])
        t_dl = V(lambda e: e.tensor_tensor(delta_t, wa, wb, ALU.add), [tl1])
        t_du = V(lambda e: e.tensor_tensor(w_du, delta_t, xn_t, ALU.mult), [t_dudma])
        t_dudma = G(lambda e, r0=r0, r1=r1: e.dma_start(scr["du"][r0:r1, :], w_du), [t_du])
        if lc > 0:
            t_car = S(lambda e: e.copy(pdcar, C.pb[3][0:1, :]), [t_colsum, t_pdmm])
        else:
            t_car = tzero
        tpd = None
        for h in range(2):
            T(lambda e, h=h: e.matmul(C.pb[1][:, h * 512:(h + 1) * 512], C.triu,
                                      delta_t[:, h * 512:(h + 1) * 512], start=True, stop=False),
              [t_dl, tz] if h == 0 else [])
            tpd = T(lambda e, h=h: e.matmul(C.pb[1][:, h * 512:(h + 1) * 512], C.onesr,
                                            pdcar[0:1, h * 512:(h + 1) * 512], start=False, stop=True),
                    [t_car])
        t_pdmm = tpd
        tcs = None
        for h in range(2):
            tcs = T(lambda e, h=h, lc=lc: e.matmul(C.pb[3][0:1, h * 512:(h + 1) * 512], C.onesc,
                                                   delta_t[:, h * 512:(h + 1) * 512],
                                                   start=(lc == 0), stop=(lc == NLC - 1)),
                    [t_car] if h == 0 else [])
        t_colsum = tcs
        t_pdevac = V(lambda e: e.tensor_copy(w_pd, C.pb[1]), [tpd, t_pddma])
        t_pb1_free = t_pdevac
        t_pddma = Y(lambda e, r0=r0, r1=r1: e.dma_start(scr["pd"][r0:r1, :], w_pd), [t_pdevac])
        tq = emit_quantize(C, xn_t, xd_bf, ws, [t_g0dma, t_dmp])
        if DBG_DUMPS:
            t_dmp = G(lambda e, r0=r0, r1=r1: e.dma_start(scr["xd"][r0:r1, :], xd_bf), [tq])
        t_xdT = emit_transpose8(C, xd_bf, std8(xdT_bf), [tq], bf16=True, pb_idx=0)
        tf = None
        for j in range(8):
            for h in range(2):
                tf = T(lambda e, j=j, h=h: e.matmul(
                    C.pb[1][:, h * 512:(h + 1) * 512], xdT_bf[:, j * 128:(j + 1) * 128],
                    wdT_bf[:, j * 1024 + h * 512: j * 1024 + (h + 1) * 512],
                    start=(j == 0), stop=(j == 7)),
                    [t_xdT, t_pdevac] if (j == 0 and h == 0) else [])
        tf0 = V(lambda e: e.tensor_tensor(wa, C.pb[1], fb_rep, ALU.add), [tf])
        t_pb1_free = tf0
        tg0 = S(lambda e: e.activation(wb, wa, AF.Silu), [tf0])
        t_g0dma = Y(lambda e, r0=r0, r1=r1: e.dma_start(scr["g0"][r0:r1, :], wb), [tg0])

    return [t_g0dma, t_pddma, t_dudma, t_xndma, t_colsum, t_xdbl]


def emit_phaseB(C, din, scr, waits0):
    P, ar = C.P, C.ar
    V = lambda fn, w=(), f=False: P.op("vector", fn, waits=w, fence=f)
    S = lambda fn, w=(): P.op("scalar", fn, waits=w)
    T = lambda fn, w=(): P.op("tensor", fn, waits=w)
    Y = lambda fn, w=(): P.op("sync", fn, waits=w, dma=True)
    G = lambda fn, w=(): P.op("gpsimd", fn, waits=w, dma=True)

    posA = ar.afp(16 * 1024)
    TdRep = ar.afp(1024); DvRep = ar.afp(1024); boutRep = ar.afp(1024)
    carr_sb = ar.afp(1024, parts=16)
    slot = [ar.afp(1024, parts=1), ar.afp(1024, parts=1)]
    arow = slot
    alog = ar.afp(128); alogT = ar.afp(128)
    bc_t = [ar.afp(32), ar.afp(32)]
    pd0 = ar.afp(1024); pd_t = [pd0, pd0]
    du_t = [ar.afp(1024), ar.afp(1024)]
    R_t = [ar.afp(1024), ar.afp(1024)]
    sp0 = ar.afp(1024); Sp_t = [sp0, sp0]
    E_t = [ar.afp(1024), ar.afp(1024)]
    q_t = [ar.afp(1024), ar.afp(1024)]
    rr0 = ar.afp(1024); r_t = [rr0, rr0]
    xn_t = ar.afp(1024); x_t = ar.afp(1024); g0_t = ar.afp(1024)
    row1 = xn_t[0:1, :]
    y_acc = ar.afp(1024); zt = ar.afp(1024); s0_t = ar.afp(1024); x1_t = ar.afp(1024)
    xs_t = ar.afp(1024)
    wout_bf = ar.abf(8192); zb = ar.abf(1024); zT_bf = ar.abf(1024)

    t_wout = G(lambda e: e.dma_start(wout_bf.rearrange("p (t e) -> p t e", e=1024),
                                     din["L0_attn_W_out"].rearrange("(t p) e -> p t e", p=128)), waits0)
    reps = {}
    tprev = None
    for name, key, rep in [("td", None, TdRep), ("dv", "L0_attn_D", DvRep),
                           ("bo", "L0_attn_b_out", boutRep)]:
        if key is None:
            tl = Y(lambda e: e.dma_start(row1, scr["pd"][L - 1:L, :]), list(waits0))
        else:
            tl = Y(lambda e, key=key: e.dma_start(row1, din[key].rearrange("(o c) -> o c", o=1)),
                   [tprev])
        tvh = None
        for h in range(2):
            tm = T(lambda e, h=h: e.matmul(C.pb[0][:, h * 512:(h + 1) * 512], C.onesr,
                                           row1[:, h * 512:(h + 1) * 512], start=True, stop=True),
                   [tl] + list(waits0) + ([tvh] if tvh else []))
            tvh = V(lambda e, rep=rep, h=h: e.tensor_copy(rep[:, h * 512:(h + 1) * 512],
                                                          C.pb[0][:, h * 512:(h + 1) * 512]), [tm])
        tprev = tvh
        reps[name] = tvh
    tla = Y(lambda e: e.dma_start(alog.rearrange("p (t n) -> p t n", n=16),
                                  din["L0_attn_A_log"].rearrange("(t p) n -> p t n", p=128)), waits0)
    tta = T(lambda e: e.transpose(C.pb[0][:, 0:128], alog, C.iden), [tla, tprev])
    tva = V(lambda e: e.tensor_copy(alogT, C.pb[0][:, 0:128]), [tta])
    tga = G(lambda e: e.dma_start(scr["alogt"], alogT), [tva])
    t_posA = []
    tprevA = [tva, tva]
    for n in range(16):
        p2 = n % 2
        tl = Y(lambda e, n=n, p2=p2: e.dma_start(
            arow[p2].rearrange("o (t p) -> o t p", p=128),
            scr["alogt"].rearrange("(t n) p -> n t p", n=16)[n:n + 1]), [tga, tprevA[p2]])
        tvh = None
        for h in range(2):
            tm = T(lambda e, h=h, p2=p2: e.matmul(C.pb[0][:, h * 512:(h + 1) * 512], C.onesr,
                                                  arow[p2][:, h * 512:(h + 1) * 512],
                                                  start=True, stop=True), [tl] + ([tvh] if tvh else []))
            tvh = S(lambda e, n=n, h=h: e.activation(
                posA[:, n * 1024 + h * 512: n * 1024 + (h + 1) * 512],
                C.pb[0][:, h * 512:(h + 1) * 512], AF.Exp), [tm])
        tprevA[p2] = tvh
        t_posA.append(tvh)
    tz0 = V(lambda e: e.memset(carr_sb, 0.0), waits0)
    t_car0 = G(lambda e: e.dma_start(scr["carr"], carr_sb), [tz0])

    numbank = [C.pb[1], C.pb[2]]
    tk = {nm: {} for nm in ["Sp", "E", "q", "num", "cols", "xs", "slot", "r"]}
    g = lambda nm, k: tk[nm].get(k)
    t_scar = {-1: t_car0}
    t_ldpd = {}; t_lddu = {}; t_ldbc = {}
    t_R = {}
    t_y = None; t_z2 = None; t_s0evac = {}; t_x1dma = None
    t_eld = {"xn": reps["bo"], "x": None, "g0": None}
    t_tr = t_wout

    def qlast(lc):
        return g("q", lc * 16 + 15)

    def splast(lc):
        return g("Sp", lc * 16 + 15)

    def lds(lc):
        if lc >= NLC:
            return
        r0, r1 = lc * 128, (lc + 1) * 128
        p = lc % 2
        t_ldpd[lc] = Y(lambda e, p=p, r0=r0, r1=r1: e.dma_start(pd_t[p], scr["pd"][r0:r1, :]),
                       [t_R.get(lc - 1), splast(lc - 1)] + (list(waits0) if lc < 2 else []))
        t_lddu[lc] = Y(lambda e, p=p, r0=r0, r1=r1: e.dma_start(du_t[p], scr["du"][r0:r1, :]),
                       [qlast(lc - 2)])
        t_ldbc[lc] = Y(lambda e, p=p, r0=r0, r1=r1: e.dma_start(bc_t[p], scr["xdbl"][r0:r1, 64:96]),
                       [qlast(lc - 2), t_y if lc >= 2 else None])

    lds(0); lds(1)
    for lc in range(NLC):
        p = lc % 2
        r0, r1 = lc * 128, (lc + 1) * 128
        t_R[lc] = V(lambda e, p=p: e.tensor_tensor(R_t[p], TdRep, pd_t[p], ALU.subtract),
                    [t_ldpd[lc], reps["td"]])
        if lc > 0:
            t_cpy = S(lambda e: e.copy(carr_sb, C.pb[3][0:16, :]),
                      [g("cols", lc * 16 - 1), t_scar[lc - 1]])
            t_scar[lc] = G(lambda e: e.dma_start(scr["carr"], carr_sb), [t_cpy])
        else:
            t_scar[0] = t_car0
        for n in range(16):
            k = lc * 16 + n
            np_ = n % 2
            tk["slot"][k] = G(lambda e, n=n, np_=np_: e.dma_start(slot[np_], scr["carr"][n:n + 1, :]),
                              [t_scar[lc], g("num", k - 2)] + ([t_posA[15]] if k < 2 else []))
            tk["Sp"][k] = V(lambda e, n=n, np_=np_, p=p: e.tensor_tensor(
                Sp_t[np_], posA[:, n * 1024:(n + 1) * 1024], R_t[p], ALU.mult),
                [t_R[lc], t_posA[n], g("E", k - 1)])
            tk["E"][k] = S(lambda e, np_=np_: e.activation(E_t[np_], Sp_t[np_], AF.Exp, scale=-1.0),
                           [tk["Sp"][k], g("q", k - 2)])
            tk["q"][k] = V(lambda e, np_=np_, p=p, n=n: e.scalar_tensor_tensor(
                q_t[np_], du_t[p], bc_t[p][:, n:n + 1], E_t[np_], ALU.mult, ALU.mult),
                [tk["E"][k], t_lddu[lc], t_ldbc[lc], g("cols", k - 2)])
            trl = S(lambda e, np_=np_: e.activation(r_t[np_], E_t[np_], AF.Ln, bias=C.ccol[:, 1:2]),
                    [g("xs", k - 1)])
            tk["r"][k] = P.op("scalar", lambda e, np_=np_: e.activation(r_t[np_], r_t[np_], AF.Exp,
                                                                        scale=-1.0),
                              waits=[trl], fence=True)
            bank = numbank[np_]
            tn = None
            w0 = [tk["q"][k], g("xs", k - 2)]
            if np_ == 1:
                w0.append(t_s0evac.get(lc - 1))
            for h in range(2):
                T(lambda e, h=h, np_=np_, bank=bank: e.matmul(
                    bank[:, h * 512:(h + 1) * 512], C.triu, q_t[np_][:, h * 512:(h + 1) * 512],
                    start=True, stop=False), w0 if h == 0 else [])
                tn = T(lambda e, h=h, np_=np_: e.matmul(
                    numbank[np_][:, h * 512:(h + 1) * 512], C.onesr,
                    slot[np_][0:1, h * 512:(h + 1) * 512], start=False, stop=True),
                    [tk["slot"][k]])
            tk["num"][k] = tn
            tc = None
            for h in range(2):
                tc = T(lambda e, h=h, n=n, np_=np_, lc=lc: e.matmul(
                    C.pb[3][0:16, h * 512:(h + 1) * 512], C.indic[:, n * 16:(n + 1) * 16],
                    q_t[np_][:, h * 512:(h + 1) * 512],
                    start=(lc == 0 and n == 0), stop=(lc == NLC - 1 and n == 15)),
                    [t_scar[lc]] if (n == 0 and h == 0) else [])
            tk["cols"][k] = tc
            tk["xs"][k] = V(lambda e, np_=np_, bank=bank: e.tensor_tensor(
                xs_t, bank, r_t[np_], ALU.mult), [tk["num"][k], tk["r"][k]])
            if n == 0:
                t_y = V(lambda e, p=p: e.tensor_scalar(y_acc, xs_t, bc_t[p][:, 16:17], None,
                                                       ALU.mult), [t_y, t_z2])
            else:
                t_y = V(lambda e, n=n, p=p: e.scalar_tensor_tensor(
                    y_acc, xs_t, bc_t[p][:, 16 + n:16 + n + 1], y_acc, ALU.mult, ALU.add))
        t_ep = {}
        for nm, buf in [("xn", xn_t), ("x", x_t), ("g0", g0_t)]:
            srcd = din["x"] if nm == "x" else scr[nm]
            t_ep[nm] = Y(lambda e, srcd=srcd, buf=buf, r0=r0, r1=r1: e.dma_start(
                buf, srcd[r0:r1, :]), [t_eld[nm]])
        tz1 = V(lambda e: e.tensor_tensor(zt, xn_t, DvRep, ALU.mult),
                [t_ep["xn"], reps["dv"], t_y])
        t_z2 = V(lambda e: e.tensor_tensor(zt, zt, y_acc, ALU.add))
        t_eld["xn"] = t_z2
        tzb = V(lambda e: e.tensor_copy(zb, zt))
        ttr = emit_transpose8(C, zb, std8(zT_bf), [tzb, t_tr], bf16=True, pb_idx=0)
        t_tr = ttr
        tm = None
        for j in range(8):
            for h in range(2):
                tm = T(lambda e, j=j, h=h: e.matmul(
                    C.pb[2][:, h * 512:(h + 1) * 512], zT_bf[:, j * 128:(j + 1) * 128],
                    wout_bf[:, j * 1024 + h * 512: j * 1024 + (h + 1) * 512],
                    start=(j == 0), stop=(j == 7)),
                    [ttr, t_wout, g("xs", lc * 16 + 15)] if (j == 0 and h == 0) else [])
        ts0 = V(lambda e: e.tensor_tensor(s0_t, C.pb[2], boutRep, ALU.add), [tm, reps["bo"]])
        t_s0evac[lc] = ts0
        tm1 = V(lambda e: e.tensor_tensor(s0_t, s0_t, g0_t, ALU.mult), [t_ep["g0"]])
        tx1 = V(lambda e: e.tensor_tensor(x1_t, x_t, s0_t, ALU.add), [t_ep["x"], t_x1dma])
        t_eld["g0"] = tx1; t_eld["x"] = tx1
        t_x1dma = Y(lambda e, r0=r0, r1=r1: e.dma_start(scr["x1"][r0:r1, :], x1_t), [tx1])
        lds(lc + 2)

    return [t_x1dma, t_s0evac[NLC - 1], tk["cols"][NLC * 16 - 1], t_scar[NLC - 1]]


def emit_phaseC(C, din, scr, waits0):
    P, ar = C.P, C.ar
    V = lambda fn, w=(), f=False: P.op("vector", fn, waits=w, fence=f)
    S = lambda fn, w=(): P.op("scalar", fn, waits=w)
    T = lambda fn, w=(): P.op("tensor", fn, waits=w)
    Y = lambda fn, w=(): P.op("sync", fn, waits=w, dma=True)
    G = lambda fn, w=(): P.op("gpsimd", fn, waits=w, dma=True)

    x1_t = [ar.afp(1024), ar.afp(1024)]
    xc = ar.afp(1024); xn1 = ar.afp(1024); nt = ar.afp(1024)
    xnTf = ar.afp(1024); ntTf = ar.afp(1024)
    w1 = ar.afp(1024)
    nwRep = ar.afp(1024)
    rw_sb = ar.afp(1024, parts=8)
    rwT = ar.afp(64)
    mcol = ar.afp(1); mcol2 = ar.afp(1); vcol = ar.afp(1); rstd = ar.afp(1)
    pr = ar.afp(8); m1 = ar.afp(1); m2 = ar.afp(1); eq1 = ar.afp(8); eq2 = ar.afp(8)
    pm = ar.afp(8); dcol = ar.afp(1); drc = ar.afp(1); comb_t = ar.afp(8)
    xnT_bf = ar.abf(1024); ntT_bf = ar.abf(1024)

    hview = lambda t, lc: scr[t].rearrange("p (t l) -> p t l", l=L)[:, :, lc * 128:(lc + 1) * 128]

    trow = Y(lambda e: e.dma_start(w1[0:1, :], din["L1_ffn_norm_weight"].rearrange("(o c) -> o c", o=1)),
             waits0)
    tvh = None
    for h in range(2):
        tm = T(lambda e, h=h: e.matmul(C.pb[0][:, h * 512:(h + 1) * 512], C.onesr,
                                       w1[0:1, h * 512:(h + 1) * 512], start=True, stop=True),
               [trow] + list(waits0) + ([tvh] if tvh else []))
        tvh = V(lambda e, h=h: e.tensor_copy(nwRep[:, h * 512:(h + 1) * 512],
                                             C.pb[0][:, h * 512:(h + 1) * 512]), [tm])
    t_nw = tvh
    trw = Y(lambda e: e.dma_start(rw_sb, din["L1_ffn_router_w"]), waits0)
    t_rwT = None
    for j in range(8):
        hh = j % 2
        tt = T(lambda e, j=j, hh=hh: e.transpose(C.pb[1][0:128, hh * 512:hh * 512 + 8],
                                                 rw_sb[0:8, j * 128:(j + 1) * 128], C.iden[0:8, 0:8]),
               [trw, t_nw] + ([t_rwT] if t_rwT else []))
        t_rwT = V(lambda e, j=j, hh=hh: e.tensor_copy(rwT[:, j * 8:(j + 1) * 8],
                                                      C.pb[1][0:128, hh * 512:hh * 512 + 8]), [tt])

    t_ldx = {}
    t_con = [None, None]
    t_hd = {"xn": None, "nt": None}
    t_tr = t_rwT
    t_comb = None
    t_prdone = None

    def ld(lc):
        if lc >= NLC:
            return
        p = lc % 2
        t_ldx[lc] = Y(lambda e, p=p, lc=lc: e.dma_start(x1_t[p], scr["x1"][lc * 128:(lc + 1) * 128, :]),
                      [t_con[p]])

    ld(0); ld(1)
    for lc in range(NLC):
        p = lc % 2
        t1 = V(lambda e, p=p: e.reduce_sum(mcol, x1_t[p], axis=AX.X), [t_ldx[lc]])
        tms = S(lambda e: e.mul(mcol2, mcol, 1.0 / 1024.0), [t1])
        t_xc = V(lambda e, p=p: e.tensor_scalar(xc, x1_t[p], mcol2, None, ALU.subtract), [tms])
        t_con[p] = t_xc
        V(lambda e: e.tensor_tensor(w1, xc, xc, ALU.mult))
        tvs = V(lambda e: e.reduce_sum(vcol, w1, axis=AX.X))
        tl1 = S(lambda e: e.activation(rstd, vcol, AF.Ln, bias=C.ccol[:, 0:1], scale=1.0 / 1024.0), [tvs])
        tst = P.op("scalar", lambda e: e.activation(rstd, rstd, AF.Exp, scale=-0.5),
                   waits=[tl1], fence=True)
        t_xn1 = V(lambda e: e.tensor_scalar(xn1, xc, rstd, None, ALU.mult), [tst, t_hd["xn"]])
        V(lambda e: e.tensor_tensor(w1, xn1, xn1, ALU.mult))
        tvs2 = V(lambda e: e.reduce_sum(vcol, w1, axis=AX.X))
        tl2 = S(lambda e: e.activation(rstd, vcol, AF.Ln, bias=C.ccol[:, 3:4], scale=1.0 / 1024.0), [tvs2])
        tst2 = P.op("scalar", lambda e: e.activation(rstd, rstd, AF.Exp, scale=-0.5),
                    waits=[tl2], fence=True)
        tn1 = V(lambda e: e.tensor_scalar(nt, xn1, rstd, None, ALU.mult), [tst2, t_hd["nt"]])
        tn2 = V(lambda e: e.tensor_tensor(nt, nt, nwRep, ALU.mult), [t_nw])
        ttr1 = emit_transpose8(C, xn1, std8(xnTf), [t_xn1, t_tr], bf16=False, pb_idx=0)
        tc1 = V(lambda e: e.tensor_copy(xnT_bf, xnTf), [ttr1])
        t_hd["xn"] = G(lambda e, lc=lc: e.dma_start(
            hview("xn1t_hbm", lc), xnT_bf.rearrange("p (t l) -> p t l", l=128)), [tc1])
        ttr2 = emit_transpose8(C, nt, std8(ntTf), [tn2, tc1, t_prdone], bf16=False, pb_idx=0)
        t_tr = ttr2
        tc2 = V(lambda e: e.tensor_copy(ntT_bf, ntTf), [ttr2])
        t_hd["nt"] = G(lambda e, lc=lc: e.dma_start(
            hview("normt_hbm", lc), ntT_bf.rearrange("p (t l) -> p t l", l=128)), [tc2])
        tmm = None
        for j in range(8):
            tmm = T(lambda e, j=j: e.matmul(C.pb[2][:, 0:8], ntTf[:, j * 128:(j + 1) * 128],
                                            rwT[:, j * 8:(j + 1) * 8], start=(j == 0), stop=(j == 7)),
                    [ttr2, t_rwT, t_prdone] if j == 0 else [])
        tpr = S(lambda e: e.activation(pr, C.pb[2][:, 0:8], AF.Sigmoid), [tmm])
        t_prdone = tpr
        tA = V(lambda e: e.reduce_max(m1, pr, axis=AX.X), [tpr])
        tB = P.op("vector", lambda e: e.tensor_scalar(eq1, pr, m1, None, ALU.is_equal),
                  waits=[tA], fence=True)
        tC = P.op("vector", lambda e: e.scalar_tensor_tensor(pm, eq1, -1e30, pr, ALU.mult, ALU.add),
                  waits=[tB], fence=True)
        tD = P.op("vector", lambda e: e.reduce_max(m2, pm, axis=AX.X), waits=[tC], fence=True)
        tE = P.op("vector", lambda e: e.tensor_scalar(eq2, pm, m2, None, ALU.is_equal),
                  waits=[tD], fence=True)
        tF = P.op("vector", lambda e: e.tensor_tensor(dcol, m1, m2, ALU.add), waits=[tD], fence=True)
        tG2 = P.op("vector", lambda e: e.tensor_scalar(dcol, dcol, 1e-6, None, ALU.add),
                   waits=[tF], fence=True)
        tH = S(lambda e: e.activation(drc, dcol, AF.Ln), [tG2])
        tI = P.op("scalar", lambda e: e.activation(drc, drc, AF.Exp, scale=-1.0),
                  waits=[tH], fence=True)
        tJ = V(lambda e: e.tensor_tensor(w1[:, 0:1], m1, drc, ALU.mult), [tI])
        tK = V(lambda e: e.tensor_tensor(w1[:, 1:2], m2, drc, ALU.mult))
        tL = P.op("vector", lambda e: e.tensor_scalar(comb_t, eq1, w1[:, 0:1], None, ALU.mult),
                  waits=[tK, tE, t_comb], fence=True)
        tM = P.op("vector", lambda e: e.scalar_tensor_tensor(comb_t, eq2, w1[:, 1:2], comb_t,
                                                             ALU.mult, ALU.add), waits=[tL], fence=True)
        t_comb = Y(lambda e, lc=lc: e.dma_start(scr["comb"][lc * 128:(lc + 1) * 128, :], comb_t), [tM])
        ld(lc + 2)

    return [t_comb, t_hd["xn"], t_hd["nt"]]


def emit_phaseC2(C, din, scr, waits0):
    P, ar = C.P, C.ar
    V = lambda fn, w=(), f=False: P.op("vector", fn, waits=w, fence=f)
    S = lambda fn, w=(): P.op("scalar", fn, waits=w)
    T = lambda fn, w=(): P.op("tensor", fn, waits=w)
    Y = lambda fn, w=(): P.op("sync", fn, waits=w, dma=True)
    G = lambda fn, w=(): P.op("gpsimd", fn, waits=w, dma=True)

    xn1T = ar.abf(8 * 4096)
    gatT = ar.abf(8 * 4096)
    wo_bf = ar.abf(8192)
    wq_bf = ar.abf(512); wk_bf = ar.abf(512); wv_bf = ar.abf(512); wg_bf = ar.abf(512)
    _qv = ar.abf(4096); qT = _qv[0:64, :]; vT = _qv[64:128, :]
    _kg = ar.abf(4096); kT = _kg[0:64, :]; gT = _kg[64:128, :]
    Vb = [ar.abf(66, parts=128), ar.abf(66, parts=128)]
    Kb = [ar.abf(64, parts=128), ar.abf(64, parts=128)]
    Sb = ar.abf(66, parts=64)
    Am = ar.abf(128)
    tq1 = ar.afp(512, parts=64); tq2 = ar.afp(512, parts=64)
    _ng = ar.afp(128); nm_t = _ng[0:64, :]; gt1 = _ng[64:128, :]
    dn_t = ar.afp(128, parts=1); dr_t = ar.afp(128, parts=1)
    s1_t = ar.afp(1024)
    onescol_bf = ar.abf(2)

    t_ld0 = G(lambda e: e.dma_start(xn1T, scr["xn1t_hbm"]), waits0)
    t_wo = G(lambda e: e.dma_start(wo_bf.rearrange("p (t e) -> p t e", e=1024),
                                   din["L1_attn_w_out"].rearrange("(t p) e -> p t e", p=128)),
             waits0)
    tob = V(lambda e: e.memset(onescol_bf, 1.0), waits0)
    tv0 = V(lambda e: e.tensor_copy(Vb[0][:, 64:65], onescol_bf[:, 0:1]), [tob])
    tv1 = V(lambda e: e.tensor_copy(Vb[1][:, 64:65], onescol_bf[:, 0:1]))

    t_wld = [None] * 4
    t_qkvg = {}
    t_h = {}
    t_am = None; t_nm = None; t_gat = None
    t_vtr = [tv0, tv1]
    t_ktr = [None, None]
    for h in range(C2_HEADS):
        if C2_HEADBAR and h > 0:
            P.barrier([t_am, t_nm, t_gat, t_h.get(h - 1)])
        offs = [h * 64, 1024 + h * 64, 2048 + h * 64]
        dsts = [wq_bf, wk_bf, wv_bf]
        tl = []
        for i in range(3):
            tl.append(G(lambda e, i=i, o=offs[i]: e.dma_start(
                dsts[i].rearrange("p (t e) -> p t e", e=64),
                din["L1_attn_w_qkv"][:, o:o + 64].rearrange("(t p) e -> p t e", p=128)),
                [t_wld[i]]))
        tl.append(G(lambda e, h=h: e.dma_start(
            wg_bf.rearrange("p (t e) -> p t e", e=64),
            din["L1_attn_w_gate"][:, h * 64:h * 64 + 64].rearrange("(t p) e -> p t e", p=128)),
            [t_wld[3]]))
        tend = {}
        tbank = [t_am, t_nm]
        for ti, (wsb, dst) in enumerate([(wq_bf, qT), (wk_bf, kT), (wv_bf, vT), (wg_bf, gT)]):
            tprev = None
            for lg in range(8):
                tmm = None
                for j in range(8):
                    bank = C.pb[2][0:64, (lg % 2) * 512:(lg % 2 + 1) * 512]
                    tmm = T(lambda e, j=j, lg=lg, wsb=wsb, bank=bank: e.matmul(
                        bank, wsb[:, j * 64:(j + 1) * 64],
                        xn1T[:, j * 4096 + lg * 512: j * 4096 + (lg + 1) * 512],
                        start=(j == 0), stop=(j == 7)),
                        ([tl[ti], t_ld0, t_qkvg.get((h - 1, ti, lg)), tbank[lg % 2]]
                         if j == 0 else []))
                bank = C.pb[2][0:64, (lg % 2) * 512:(lg % 2 + 1) * 512]
                sl = slice(lg * 512, (lg + 1) * 512)
                if ti < 2:
                    ta = V(lambda e, bank=bank: e.tensor_scalar(tq1, bank, 0.0, None, ALU.min), [tmm])
                    tb = S(lambda e: e.activation(tq1, tq1, AF.Exp), [ta])
                    tc = V(lambda e, bank=bank: e.tensor_scalar(tq2, bank, 0.0, None, ALU.max), [tb])
                    tev = V(lambda e, dst=dst, sl=sl: e.tensor_tensor(dst[:, sl], tq1, tq2, ALU.add))
                elif ti == 3:
                    tev = S(lambda e, dst=dst, sl=sl, bank=bank: e.activation(dst[:, sl], bank,
                                                                             AF.Sigmoid), [tmm])
                else:
                    tev = V(lambda e, dst=dst, sl=sl, bank=bank: e.tensor_copy(dst[:, sl], bank), [tmm])
                t_qkvg[(h, ti, lg)] = tev
                tbank[lg % 2] = tev
                tprev = tev
                if lg == 7:
                    t_wld[ti] = tmm
            tend[ti] = tprev
        if h == 0 and DBG_DUMPS:
            G(lambda e: e.dma_start(scr["qt0_hbm"], qT), [tend[0]])
        tsz = V(lambda e: e.memset(Sb, 0.0), [t_h.get(h - 1)])
        t_sup = None
        for lc in range(NLC):
            cp = lc % 2
            sl = slice(lc * 128, (lc + 1) * 128)
            tt1 = T(lambda e, sl=sl, cp=cp: e.transpose(C.pbf(0, cp)[:, 0:64], vT[:, sl],
                                                        C.idenb[64:128, 64:128]),
                    [tend[2], t_vtr[cp]])
            V(lambda e, cp=cp: e.tensor_copy(Vb[cp][:, 0:64], C.pbf(0, cp)[:, 0:64]), [tt1])
            tvb = P.op("vector", lambda e, cp=cp: e.tensor_copy(Vb[cp][:, 64:65],
                                                                onescol_bf[:, 0:1]),
                       waits=[tob], fence=True)
            tt2 = T(lambda e, sl=sl, cp=cp: e.transpose(C.pbf(1, cp)[:, 0:64], kT[:, sl],
                                                        C.idenb[0:64, 0:64]),
                    [tend[1], t_ktr[cp]])
            tkb = V(lambda e, cp=cp: e.tensor_copy(Kb[cp], C.pbf(1, cp)[:, 0:64]), [tt2])
            tam = T(lambda e, sl=sl: e.matmul(C.pb[2][:, 0:128], kT[:, sl], qT[:, sl],
                                              start=True, stop=True),
                    [tend[0], tend[1], t_am] + ([tend[3]] if lc == 0 else []))
            tms = V(lambda e: e.tensor_tensor(Am, C.pb[2][:, 0:128], C.triu, ALU.mult), [tam])
            t_am = tms
            if DBG_DUMPS and h == 0:
                tdmp = G(lambda e, lc=lc: e.dma_start(
                    scr["am_hbm"][:, lc * 128:(lc + 1) * 128], Am), [tms])
                tms = tdmp
            tn1 = T(lambda e, cp=cp, lc=lc: e.matmul(C.pb[2][0:65, 512:640], Vb[cp][:, 0:65], Am,
                                                     start=True, stop=(lc == 0)),
                    [tms, tvb, t_nm])
            if lc > 0:
                tn1 = T(lambda e, sl=sl: e.matmul(C.pb[2][0:65, 512:640], Sb[:, 0:65], qT[:, sl],
                                                  start=False, stop=True), [t_sup])
            td1 = V(lambda e: e.tensor_scalar(dn_t, C.pb[2][64:65, 512:640], 1e-6, None, ALU.add),
                    [tn1])
            if DBG_DUMPS and h == 0:
                tdd = Y(lambda e, lc=lc: e.dma_start(scr["den"][lc:lc + 1, :], dn_t), [td1])
                td1 = tdd
            td2 = S(lambda e: e.activation(dr_t, dn_t, AF.Ln), [td1])
            td3 = P.op("scalar", lambda e: e.activation(dr_t, dr_t, AF.Exp, scale=-1.0),
                       waits=[td2], fence=True)
            tbc = T(lambda e: e.matmul(C.pb[3][0:64, 0:128], C.onesr[0:1, 0:64], dr_t,
                                       start=True, stop=True), [td3, t_gat])
            tnm = V(lambda e: e.tensor_copy(nm_t, C.pb[2][0:64, 512:640]), [tn1])
            t_nm = tnm
            tg1 = V(lambda e: e.tensor_tensor(gt1, nm_t, C.pb[3][0:64, 0:128], ALU.mult), [tbc])
            t_gat = tg1
            hrow = (h % 2) * 64
            hcol = (h // 2) * 4096
            tg2 = V(lambda e, sl=sl, hrow=hrow, hcol=hcol, lc=lc: e.tensor_tensor(
                gatT[hrow:hrow + 64, hcol + lc * 128: hcol + (lc + 1) * 128], gt1,
                gT[:, sl], ALU.mult), [tend[3]])
            tsu = T(lambda e, cp=cp, lc=lc: e.matmul(C.pb[3][0:64, 640:705], Kb[cp],
                                                     Vb[cp][:, 0:65],
                                                     start=(lc == 0), stop=(lc == NLC - 1)),
                    [tkb, tvb, tsz] + ([t_sup] if t_sup else []))
            tsb = V(lambda e: e.tensor_copy(Sb[:, 0:65], C.pb[3][0:64, 640:705]), [tsu])
            t_sup = tsb
            t_vtr[cp] = tvb
            t_ktr[cp] = tkb
        t_h[h] = tg2
    if DBG_DUMPS:
        t_all0 = [t_h[hh] for hh in range(C2_HEADS)]
        G(lambda e: e.dma_start(scr["gatt_hbm"], gatT), t_all0)
        G(lambda e: e.dma_start(scr["qt_hbm"], qT), t_all0)
        G(lambda e: e.dma_start(scr["kt_hbm"], kT), t_all0)
        G(lambda e: e.dma_start(scr["vt_hbm"], vT), t_all0)
        G(lambda e: e.dma_start(scr["gt_hbm"], gT), t_all0)
    t_s1 = None
    tails = []
    t_all = [t_h[hh] for hh in range(C2_HEADS)]
    for lc in range(NLC):
        tmm = None
        for j in range(8):
            for hh2 in range(2):
                tmm = T(lambda e, j=j, hh2=hh2, lc=lc: e.matmul(
                    C.pb[2][:, hh2 * 512:(hh2 + 1) * 512], gatT[:, j * 4096 + lc * 128:j * 4096 + (lc + 1) * 128],
                    wo_bf[:, j * 1024 + hh2 * 512: j * 1024 + (hh2 + 1) * 512],
                    start=(j == 0), stop=(j == 7)),
                    (t_all + [t_wo, t_s1]) if (j == 0 and hh2 == 0) else [])
        tev = V(lambda e: e.tensor_copy(s1_t, C.pb[2]), [tmm])
        t_s1 = Y(lambda e, lc=lc: e.dma_start(scr["s1"][lc * 128:(lc + 1) * 128, :], s1_t), [tev])
        tails = [t_s1]
    return tails


def emit_phaseD(C, din, scr, out_d, waits0):
    P, ar = C.P, C.ar
    V = lambda fn, w=(), f=False: P.op("vector", fn, waits=w, fence=f)
    S = lambda fn, w=(): P.op("scalar", fn, waits=w)
    T = lambda fn, w=(): P.op("tensor", fn, waits=w)
    Y = lambda fn, w=(): P.op("sync", fn, waits=w, dma=True)
    G = lambda fn, w=(): P.op("gpsimd", fn, waits=w, dma=True)

    ew1_sb = ar.abf(8 * 4096)
    ew2_sb = ar.abf(32 * 1024)
    h_buf = ar.abf(32 * 512)
    normT = [ar.abf(8 * 512), ar.abf(8 * 512)]
    comb_res = ar.afp(256)
    b1col = ar.afp(32)
    b1stg = ar.afp(128, parts=32)
    b2rep = ar.afp(1024)
    wv = [ar.afp(1024), ar.afp(1024)]
    acc_t = h_buf[:, 0:2048].bitcast(F32)
    x1_t = h_buf[:, 2048:4096].bitcast(F32)
    s1_t = h_buf[:, 4096:6144].bitcast(F32)
    row1 = b2rep[0:1, :]

    tcomb = Y(lambda e: e.dma_start(comb_res.rearrange("p (t e) -> p t e", e=8),
                                    scr["comb"].rearrange("(t p) e -> p t e", p=128)), waits0)

    t_ew1last = None; t_ew2last = None
    t_b1 = None; t_b2user = None
    t_nld = [None, None]
    t_relu = {}
    t_acc = {}
    t_wvdma = [None, None]
    t_ps = {0: None, 1: None}
    t_hb = {0: None, 1: None}
    t_mm2l = None
    gls = 0

    for e in range(9):
        if e == 0:
            w1src = din["L1_ffn_sw1"]; w2src = din["L1_ffn_sw2"]
            b1src = din["L1_ffn_sb1"]; b2src = din["L1_ffn_sb2"]
        else:
            w1src = din["L1_ffn_ew1"][e - 1]; w2src = din["L1_ffn_ew2"][e - 1]
            b1src = din["L1_ffn_eb1"][e - 1]; b2src = din["L1_ffn_eb2"][e - 1]
        tw1 = G(lambda e2, w1src=w1src: e2.dma_start(
            ew1_sb.rearrange("p (t h) -> p t h", h=4096),
            w1src.rearrange("(t p) h -> p t h", p=128)), [t_ew1last] + list(waits0))
        tw2 = G(lambda e2, w2src=w2src: e2.dma_start(
            ew2_sb.rearrange("p (t o) -> p t o", o=1024),
            w2src.rearrange("(t p) o -> p t o", p=128)), [t_ew2last])
        tb1s = Y(lambda e2, b1src=b1src: e2.dma_start(
            b1stg, b1src.rearrange("(c p) -> c p", p=128)), [t_b1])
        tb1t = T(lambda e2: e2.transpose(C.pb[1][:, 0:32], b1stg, C.iden[0:32, 0:32]),
                 [tb1s, t_b2user])
        tb1 = V(lambda e2: e2.tensor_copy(b1col, C.pb[1][:, 0:32]), [tb1t])
        tb2r = Y(lambda e2, b2src=b2src: e2.dma_start(
            row1, b2src.rearrange("(o c) -> o c", o=1)), [t_b2user])
        tvh = None
        for hh in range(2):
            tmb = T(lambda e2, hh=hh: e2.matmul(C.pb[1][:, hh * 512:(hh + 1) * 512], C.onesr,
                                                row1[0:1, hh * 512:(hh + 1) * 512],
                                                start=True, stop=True),
                    [tb2r, t_b2user, tb1] + ([tvh] if tvh else []))
            tvh = V(lambda e2, hh=hh: e2.tensor_copy(b2rep[:, hh * 512:(hh + 1) * 512],
                                                     C.pb[1][:, hh * 512:(hh + 1) * 512]), [tmb])
        t_b2 = tvh
        for lg in range(8):
            lp = lg % 2
            tnl = Y(lambda e2, lg=lg, lp=lp: e2.dma_start(
                normT[lp].rearrange("p (t l) -> p t l", l=512),
                scr["normt_hbm"].rearrange("p (t l) -> p t l", l=L)[:, :, lg * 512:(lg + 1) * 512]),
                [t_nld[lp]])
            for ht in range(32):
                hb = ht % 2
                bank = C.pb[0][:, hb * 512:(hb + 1) * 512]
                tm = None
                for dt in range(8):
                    tm = T(lambda e2, dt=dt, ht=ht, lp=lp, bank=bank: e2.matmul(
                        bank, ew1_sb[:, dt * 4096 + ht * 128: dt * 4096 + (ht + 1) * 128],
                        normT[lp][:, dt * 512:(dt + 1) * 512],
                        start=(dt == 0), stop=(dt == 7)),
                        [tnl, tw1, t_hb[hb]] if dt == 0 else [])
                t_relu[ht] = S(lambda e2, ht=ht, bank=bank: e2.activation(
                    h_buf[:, ht * 512:(ht + 1) * 512], bank, AF.Relu,
                    bias=b1col[:, ht:ht + 1]), [tm, tb1])
                t_hb[hb] = t_relu[ht]
                if lg == 7 and ht == 31:
                    t_ew1last = tm
            t_rl = t_relu[31]
            for ls in range(4):
                pp = gls % 2
                psb = C.pb[2] if pp == 0 else C.pb[3]
                tm = None
                for oh in range(2):
                    for ht in range(32):
                        tm = T(lambda e2, ht=ht, ls=ls, oh=oh, psb=psb: e2.matmul(
                            psb[:, oh * 512:(oh + 1) * 512],
                            h_buf[:, ht * 512 + ls * 128: ht * 512 + (ls + 1) * 128],
                            ew2_sb[:, ht * 1024 + oh * 512: ht * 1024 + (oh + 1) * 512],
                            start=(ht == 0), stop=(ht == 31)),
                            [t_rl, tw2, t_ps[pp]] if (ht == 0 and oh == 0) else [])
                t_mm2l = tm
                r0 = lg * 512 + ls * 128
                wvb = wv[gls % 2]
                if e == 0:
                    tev = V(lambda e2, psb=psb, wvb=wvb: e2.tensor_tensor(
                        wvb, psb, b2rep, ALU.add), [tm, t_b2, t_wvdma[gls % 2]])
                    t_ps[pp] = tev
                    t_acc[(lg, ls)] = G(lambda e2, r0=r0, wvb=wvb: e2.dma_start(
                        scr["acc"][r0:r0 + 128, :], wvb), [tev])
                else:
                    cidx = (lg * 4 + ls) * 8 + (e - 1)
                    te1 = V(lambda e2, psb=psb, wvb=wvb, cidx=cidx: e2.tensor_scalar(
                        wvb, psb, comb_res[:, cidx:cidx + 1], None, ALU.mult),
                        [tm, tcomb, t_wvdma[gls % 2]])
                    tev = V(lambda e2, wvb=wvb, cidx=cidx: e2.scalar_tensor_tensor(
                        wvb, b2rep, comb_res[:, cidx:cidx + 1], wvb, ALU.mult, ALU.add),
                        [t_b2])
                    t_ps[pp] = tev
                    t_acc[(lg, ls)] = G(lambda e2, r0=r0, wvb=wvb: e2.dma_start(
                        scr["acc"][r0:r0 + 128, :], wvb, accum_op=ALU.add),
                        [tev, t_acc[(lg, ls)]])
                t_wvdma[gls % 2] = t_acc[(lg, ls)]
                gls += 1
            t_nld[lp] = tm
        t_b1 = t_rl
        t_b2user = tev
        t_ew2last = t_mm2l
    t_allacc = [t_acc[k] for k in t_acc]
    t_ldc = {"acc": None, "x1": None, "s1": None}
    t_out = None
    tails = []
    for lc in range(NLC):
        r0, r1 = lc * 128, (lc + 1) * 128
        tla = Y(lambda e2, r0=r0, r1=r1: e2.dma_start(acc_t, scr["acc"][r0:r1, :]),
                t_allacc if lc == 0 else [t_ldc["acc"]])
        tlx = Y(lambda e2, r0=r0, r1=r1: e2.dma_start(x1_t, scr["x1"][r0:r1, :]), [t_ldc["x1"]])
        tls = Y(lambda e2, r0=r0, r1=r1: e2.dma_start(s1_t, scr["s1"][r0:r1, :]), [t_ldc["s1"]])
        tg1 = S(lambda e2: e2.activation(acc_t, acc_t, AF.Silu), [tla, t_out])
        tm1 = V(lambda e2: e2.tensor_tensor(s1_t, s1_t, acc_t, ALU.mult), [tg1, tls])
        tm2 = V(lambda e2: e2.tensor_tensor(x1_t, x1_t, s1_t, ALU.add), [tlx, t_out])
        t_ldc["acc"] = tm1; t_ldc["s1"] = tm1; t_ldc["x1"] = tm2
        t_out = Y(lambda e2, r0=r0, r1=r1: e2.dma_start(out_d[r0:r1, :], x1_t), [tm2])
        tails = [t_out]
    return tails


INPUT_SPECS = {
    "x": (L, D), "L0_attn_W_x": (1024, 96), "L0_attn_W_dt": (64, 1024),
    "L0_attn_b_dt": (1024,), "L0_attn_A_log": (1024, 16), "L0_attn_D": (1024,),
    "L0_attn_W_out": (1024, 1024), "L0_attn_b_out": (1024,),
    "L0_ffn_weight": (1024, 1024), "L0_ffn_bias": (1024,),
    "L1_attn_w_qkv": (1024, 3072), "L1_attn_w_gate": (1024, 1024),
    "L1_attn_w_out": (1024, 1024), "L1_ffn_norm_weight": (1024,),
    "L1_ffn_router_w": (8, 1024), "L1_ffn_sw1": (1024, 4096), "L1_ffn_sb1": (4096,),
    "L1_ffn_sw2": (4096, 1024), "L1_ffn_sb2": (1024,),
    "L1_ffn_ew1": (8, 1024, 4096), "L1_ffn_eb1": (8, 4096),
    "L1_ffn_ew2": (8, 4096, 1024), "L1_ffn_eb2": (8, 1024),
    "c_triu": (128, 128), "c_iden": (128, 128), "c_ones_row": (1, 128),
    "c_ones_col": (128, 1), "c_indic": (128, 256),
}

SCRATCH_SPECS = {
    "xn": (L, D), "xdbl": (L, 96), "pd": (L, D), "du": (L, D), "g0": (L, D),
    "x1": (L, D), "s1": (L, D), "comb": (L, 8), "acc": (L, D),
    "xn1t_hbm": (128, NDT * L), "normt_hbm": (128, NDT * L), "carr": (16, 1024),
    "xd": (L, D), "wdq": (1024, 1024), "alogt": (128, 128),
    "gatt_hbm": (128, NDT * L), "qt_hbm": (64, L), "kt_hbm": (64, L), "qt0_hbm": (64, L),
    "vt_hbm": (64, L), "gt_hbm": (64, L), "am_hbm": (128, NLC * 128), "den": (NLC, 128),
    "num_hbm": (64, NLC * 128),
}


def build(phases=("A",), dbg=()):
    nc = bass.Bass("TRN2", target_bir_lowering=False)
    din = {}
    for name, shape in INPUT_SPECS.items():
        din[name] = nc.dram_tensor(name, list(shape), F32, kind="ExternalInput").ap()
    scr = {}
    for name, shape in SCRATCH_SPECS.items():
        kind = "ExternalOutput" if name in dbg else "Internal"
        dt = BF16 if name.endswith("_hbm") else F32
        scr[name] = nc.dram_tensor("s_" + name, list(shape), dt, kind=kind).ap()
    out_d = nc.dram_tensor("out", [L, D], F32, kind="ExternalOutput").ap()

    P = Prog(nc)
    ar = Arena(nc, 48600)
    C = Ctx(nc, P, ar)
    with nc.Block() as block:
        t0 = emit_const_loads(C, din)
        tails = emit_phaseA(C, din, scr, [t0])
        if "B" in phases:
            P.barrier(tails)
            ar.reset()
            tails = emit_phaseB(C, din, scr, [])
        if "C" in phases:
            P.barrier(tails)
            ar.reset()
            tails = emit_phaseC(C, din, scr, [])
        if "C2" in phases:
            P.barrier(tails)
            ar.reset()
            tails = emit_phaseC2(C, din, scr, [])
        if "D" in phases:
            P.barrier(tails)
            ar.reset()
            tails = emit_phaseD(C, din, scr, out_d, [])
        P.barrier(tails)
        P.finalize(block)
    return nc




def kernel(**inputs):
    from concourse.bass_utils import run_bass_kernel_spmd
    x = np.ascontiguousarray(inputs["L0_attn_x"])
    nc = build(phases=("A", "B", "C", "C2", "D"), dbg=())
    consts = host_constants()
    base = {k: np.ascontiguousarray(v) for k, v in inputs.items() if k != "L0_attn_x"}
    base.update(consts)
    in_maps = []
    for b in range(8):
        m = dict(base)
        m["x"] = np.ascontiguousarray(x[b])
        in_maps.append(m)
    res = run_bass_kernel_spmd(nc, in_maps, core_ids=list(range(8)))
    out = np.stack([np.asarray(res.results[b]["out"]) for b in range(8)], axis=0)
    return out.astype(np.float32)


# version 05
# speedup vs baseline: 1.2405x; 1.2405x over previous
import numpy as np
import concourse.bass as bass
import concourse.mybir as mybir

F32 = mybir.dt.float32
F32R = mybir.dt.float32r
BF16 = mybir.dt.bfloat16
I32 = mybir.dt.int32
AF = mybir.ActivationFunctionType
ALU = mybir.AluOpType
AX = mybir.AxisListType

L = 4096
D = 1024
NLC = 32
NDT = 8
DTR = 64
NS = 16

ENGS = ("tensor", "vector", "scalar", "gpsimd", "sync")
C2_HEADS = 16
C2_HEADBAR = False
DBG_DUMPS = False


class Tok:
    __slots__ = ("eng", "idx", "dma")
    def __init__(self, eng, idx, dma):
        self.eng = eng; self.idx = idx; self.dma = dma


class Prog:
    def __init__(self, nc):
        self.nc = nc
        self.records = {e: [] for e in ENGS}
        self.sems = {}

    def op(self, eng, fn, waits=(), dma=False, fence=False):
        ws = [w for w in waits if w is not None and (fence or w.dma or w.eng != eng)]
        for w in ws:
            self.records[w.eng][w.idx][3][0] = True
        self.records[eng].append([fn, ws, dma, [bool(dma)]])
        return Tok(eng, len(self.records[eng]) - 1, dma)

    def barrier(self, toks):
        for e in ENGS:
            self.op(e, None, waits=toks)

    DPOOL = 24

    def finalize(self, block):
        for e in ENGS:
            self.sems[e] = self.nc.alloc_semaphore(f"es_{e}")
        dpool = {}
        for e in ("sync", "gpsimd"):
            dpool[e] = [self.nc.alloc_semaphore(f"dp_{e}_{i}") for i in range(self.DPOOL)]
        rank = {e: [] for e in ENGS}
        dinfo = {e: [] for e in ENGS}
        for e in ENGS:
            r = 0; dcount = 0
            for fn, ws, dma, marked in self.records[e]:
                if dma:
                    pi = dcount % self.DPOOL
                    val = 16 * (dcount // self.DPOOL + 1)
                    dinfo[e].append((pi, val))
                    dcount += 1
                else:
                    dinfo[e].append(None)
                    if marked[0]:
                        r += 1
                rank[e].append(r)
        sems = self.sems

        def section(e):
            def body(eng):
                for i, (fn, ws, dma, marked) in enumerate(self.records[e]):
                    for w in ws:
                        if w.dma:
                            pi, val = dinfo[w.eng][w.idx]
                            eng.wait_ge(dpool[w.eng][pi], val)
                        else:
                            eng.wait_ge(sems[w.eng], rank[w.eng][w.idx])
                    if fn is None:
                        continue
                    if dma:
                        pi, val = dinfo[e][i]
                        if val > 16:
                            eng.wait_ge(dpool[e][pi], val - 16)
                        fn(eng).then_inc(dpool[e][pi], 16)
                    else:
                        ins = fn(eng)
                        if marked[0]:
                            ins.then_inc(sems[e], 1)
            return body
        for e in ENGS:
            getattr(block, e)(section(e))


class Arena:
    def __init__(self, nc, f32_cols, bf16_cols=0):
        self.f32t = nc.alloc_sbuf_tensor("arena_f32", [128, f32_cols], F32).ap()
        self.f32_cols = f32_cols
        self.reset()

    def reset(self):
        self.f32_off = 0

    def afp(self, cols, parts=128):
        o = self.f32_off; self.f32_off += cols
        assert self.f32_off <= self.f32_cols, f"f32 arena overflow {self.f32_off}"
        return self.f32t[0:parts, o:o + cols]

    def abf(self, cols, parts=128):
        n = (cols + 1) // 2
        o = self.f32_off; self.f32_off += n
        assert self.f32_off <= self.f32_cols, f"f32 arena overflow {self.f32_off}"
        return self.f32t[:, o:o + n].bitcast(BF16)[0:parts, 0:cols]


def host_constants():
    triu = np.triu(np.ones((128, 128), dtype=np.float32))
    iden = np.eye(128, dtype=np.float32)
    ones_row = np.ones((1, 128), dtype=np.float32)
    ones_col = np.ones((128, 1), dtype=np.float32)
    indic = np.zeros((128, 16 * 16), dtype=np.float32)
    for n in range(16):
        indic[:, 16 * n + n] = 1.0
    return {"c_triu": triu, "c_iden": iden, "c_ones_row": ones_row,
            "c_ones_col": ones_col, "c_indic": indic}


class Ctx:
    def __init__(self, nc, P, ar):
        self.nc, self.P, self.ar = nc, P, ar
        alloc = nc.alloc_sbuf_tensor
        self.triu = alloc("triu", [128, 128], F32).ap()
        self.iden = alloc("iden", [128, 128], F32).ap()
        self.idenb = alloc("idenb", [128, 128], BF16).ap()
        self.onesr = alloc("onesr", [1, 128], F32).ap()
        self.onesc = alloc("onesc", [128, 1], F32).ap()
        self.indic = alloc("indic", [128, 256], F32).ap()
        self.int_t = alloc("int_t", [128, 1024], I32).ap()
        self.ccol = alloc("ccol", [128, 4], F32).ap()
        self.pb = [nc.alloc_psum_tensor(f"pb{i}", [128, 1024], F32).ap()
                   for i in range(4)]

    def pbf(self, i, half):
        return self.pb[i][:, half * 512:half * 512 + 64].bitcast(BF16)


def emit_const_loads(C, dins):
    P = C.P
    t = None
    for dst, src in [(C.triu, dins["c_triu"]), (C.iden, dins["c_iden"]),
                     (C.onesr, dins["c_ones_row"]), (C.onesc, dins["c_ones_col"]),
                     (C.indic, dins["c_indic"])]:
        t = P.op("sync", lambda e, d=dst, s=src: e.dma_start(d, s), dma=True)
    tb = P.op("vector", lambda e: e.tensor_copy(C.idenb, C.iden), waits=[t])
    P.op("vector", lambda e: e.memset(C.ccol[:, 0:1], 1e-5))
    P.op("vector", lambda e: e.memset(C.ccol[:, 1:2], 1e-12))
    P.op("vector", lambda e: e.memset(C.ccol[:, 2:3], 1e-6))
    tb = P.op("vector", lambda e: e.memset(C.ccol[:, 3:4], 1e-8))
    return tb


def emit_quantize(C, src, dst_bf, ws, waits):
    P = C.P
    V = lambda fn, w=(): P.op("vector", fn, waits=w)
    S = lambda fn, w=(): P.op("scalar", fn, waits=w)
    wa, wb, wc, wd = ws["wa"], ws["wb"], ws["wc"], ws["wd"]
    sc, inv = ws["sc"], ws["inv"]
    v = lambda ap, w: ap.rearrange("p (r c) -> p r c", c=w)
    t6 = V(lambda e: e.reduce_max(wb[:, 0:32].unsqueeze(2), v(src, 32), axis=AX.X,
                                  apply_absolute_value=True), waits)
    t7 = P.op("vector", lambda e: e.tensor_scalar(sc, wb[:, 0:32], 1.0 / 7.5, 1e-30,
                                                  ALU.mult, ALU.max), waits=[t6], fence=True)
    t8a = S(lambda e: e.activation(inv, sc, AF.Ln), [t7])
    t8 = P.op("scalar", lambda e: e.activation(inv, inv, AF.Exp, scale=-1.0),
              waits=[t8a], fence=True)
    bc = lambda ap: ap.unsqueeze(2).broadcast_to([128, 32, 32])
    t9 = V(lambda e: e.tensor_tensor(v(wa, 32), v(src, 32), bc(inv), ALU.mult), [t8])
    ta = V(lambda e: e.tensor_scalar(wb, wa, -1.0, None, ALU.mult), [t9])
    tb = V(lambda e: e.tensor_tensor(wb, wa, wb, ALU.max), [ta])
    tc = V(lambda e: e.tensor_scalar(wc, wb, 2.0, None, ALU.is_ge), [tb])
    td = V(lambda e: e.tensor_scalar(wd, wb, 4.0, None, ALU.is_ge), [tc])
    te = V(lambda e: e.tensor_scalar(wb, wc, -4.0, 8.0, ALU.mult, ALU.add), [td])
    tf = V(lambda e: e.scalar_tensor_tensor(wb, wd, -2.0, wb, ALU.mult, ALU.add), [te])
    tg = V(lambda e: e.tensor_scalar(wc, wc, 0.125, 0.125, ALU.mult, ALU.add), [tf])
    th = V(lambda e: e.scalar_tensor_tensor(wc, wd, 0.25, wc, ALU.mult, ALU.add), [tg])
    ti = V(lambda e: e.tensor_tensor(wa, wa, wb, ALU.mult), [th])
    tj = V(lambda e: e.tensor_copy(C.int_t, wa), [ti])
    tk = V(lambda e: e.tensor_copy(wa, C.int_t), [tj])
    tl = V(lambda e: e.tensor_tensor(wa, wa, wc, ALU.mult), [tk])
    tm = V(lambda e: e.tensor_tensor(v(dst_bf, 32), v(wa, 32), bc(sc), ALU.mult), [tl])
    return tm


def emit_transpose8(C, src, dst_slices, waits, bf16=False, pb_idx=0):
    P = C.P
    last_v = {0: None, 1: None}
    tv = None
    for j in range(8):
        h = j % 2
        if bf16:
            pst = C.pbf(pb_idx, h)
        else:
            pst = C.pb[pb_idx][:, h * 512:h * 512 + 128]
        iden = C.idenb if bf16 else C.iden
        w = list(waits) + ([last_v[h]] if last_v[h] else [])
        tt = P.op("tensor", lambda e, j=j, pst=pst, iden=iden:
                  e.transpose(pst, src[:, j * 128:(j + 1) * 128], iden), waits=w)
        tv = P.op("vector", lambda e, j=j, pst=pst:
                  e.tensor_copy(dst_slices[j], pst), waits=[tt])
        last_v[h] = tv
    return tv


def std8(dst):
    return [dst[:, j * 128:(j + 1) * 128] for j in range(8)]


def emit_phaseA(C, din, scr, waits0):
    P, ar = C.P, C.ar
    V = lambda fn, w=(): P.op("vector", fn, waits=w)
    S = lambda fn, w=(): P.op("scalar", fn, waits=w)
    T = lambda fn, w=(): P.op("tensor", fn, waits=w)
    Y = lambda fn, w=(): P.op("sync", fn, waits=w, dma=True)
    G = lambda fn, w=(): P.op("gpsimd", fn, waits=w, dma=True)

    Wx = ar.afp(768)
    Wdt = ar.afp(1024, parts=64)
    bdt_rep = ar.afp(1024); fb_rep = ar.afp(1024)
    bdt_row = ar.afp(1024, parts=1); fb_row = ar.afp(1024, parts=1)
    pdcar = ar.afp(1024, parts=1)
    xnT = ar.afp(1024)
    xdbl_sb = ar.afp(3072)
    dT = ar.afp(128, parts=64)
    xt = [ar.afp(1024), ar.afp(1024)]
    xn_t = ar.afp(1024); xc = ar.afp(1024); delta_t = ar.afp(1024)
    wa, wb, wc, wd = ar.afp(1024), ar.afp(1024), ar.afp(1024), ar.afp(1024)
    w_soft = ar.afp(1024); w_du = ar.afp(1024); w_pd = ar.afp(1024)
    sc, inv = ar.afp(32), ar.afp(32)
    mcol = ar.afp(1); mcol2 = ar.afp(1); vcol = ar.afp(1); rstd = ar.afp(1)
    ws = {"wa": wa, "wb": wb, "wc": wc, "wd": wd, "sc": sc, "inv": inv}
    xd_bf = ar.abf(1024); xdT_bf = ar.abf(1024); wdq_bf = ar.abf(1024)
    wdT_bf = ar.abf(8192)
    wsrc = ar.afp(1024)

    tWx = Y(lambda e: e.dma_start(Wx.rearrange("p (t e) -> p t e", e=96),
                                  din["L0_attn_W_x"].rearrange("(t p) e -> p t e", p=128)), waits0)
    tWdt = Y(lambda e: e.dma_start(Wdt, din["L0_attn_W_dt"]))
    tbdt = Y(lambda e: e.dma_start(bdt_row, din["L0_attn_b_dt"].rearrange("(o c) -> o c", o=1)))
    tfb = Y(lambda e: e.dma_start(fb_row, din["L0_ffn_bias"].rearrange("(o c) -> o c", o=1)))
    reps = []
    for (row, rep, tl) in [(bdt_row, bdt_rep, tbdt), (fb_row, fb_rep, tfb)]:
        tvh = None
        for h in range(2):
            tm = T(lambda e, row=row, h=h: e.matmul(
                C.pb[0][:, h * 512:(h + 1) * 512], C.onesr, row[:, h * 512:(h + 1) * 512],
                start=True, stop=True), [tl] + list(waits0) + ([tvh] if tvh else []))
            tvh = V(lambda e, rep=rep, h=h: e.tensor_copy(
                rep[:, h * 512:(h + 1) * 512], C.pb[0][:, h * 512:(h + 1) * 512]), [tm])
        reps.append(tvh)
    trep = reps[-1]
    tzero = V(lambda e: e.memset(pdcar, 0.0))

    ttr_w = trep
    tq_w = None
    t_dmp = None
    for ot in range(8):
        tl = Y(lambda e, ot=ot: e.dma_start(wsrc, din["L0_ffn_weight"][ot * 128:(ot + 1) * 128, :]),
               [tq_w] if tq_w else list(waits0))
        tq_w = emit_quantize(C, wsrc, wdq_bf, ws, [tl, ttr_w, t_dmp])
        if DBG_DUMPS:
            t_dmp = G(lambda e, ot=ot: e.dma_start(scr["wdq"][ot * 128:(ot + 1) * 128, :], wdq_bf), [tq_w])
        dsts = [wdT_bf[:, dt * 1024 + ot * 128: dt * 1024 + (ot + 1) * 128] for dt in range(8)]
        ttr_w = emit_transpose8(C, wdq_bf, dsts, [tq_w], bf16=True, pb_idx=0)

    tails = {}
    t_xcon = [None, None]
    t_pb1_free = ttr_w
    t_pb2a_free = None
    t_pb2b_free = None
    t_pdmm = None
    t_colsum = None
    t_g0dma = None
    t_pddma = None
    t_dudma = None
    t_xndma = None
    t_tr = ttr_w

    for lc in range(NLC):
        r0, r1 = lc * 128, (lc + 1) * 128
        bufx = xt[lc % 2]
        tldx = Y(lambda e, bufx=bufx, r0=r0, r1=r1: e.dma_start(bufx, din["x"][r0:r1, :]),
                 [t_xcon[lc % 2]])
        t1 = V(lambda e, bufx=bufx: e.reduce_sum(mcol, bufx, axis=AX.X), [tldx])
        tms = S(lambda e: e.mul(mcol2, mcol, 1.0 / 1024.0), [t1])
        t_xc = V(lambda e, bufx=bufx: e.tensor_scalar(xc, bufx, mcol2, None, ALU.subtract), [tms])
        t_xcon[lc % 2] = t_xc
        V(lambda e: e.tensor_tensor(wa, xc, xc, ALU.mult), [t_g0dma])
        tvs = V(lambda e: e.reduce_sum(vcol, wa, axis=AX.X))
        tl0 = S(lambda e: e.activation(rstd, vcol, AF.Ln, bias=C.ccol[:, 0:1], scale=1.0 / 1024.0), [tvs])
        tst = P.op("scalar", lambda e: e.activation(rstd, rstd, AF.Exp, scale=-0.5),
                   waits=[tl0], fence=True)
        t_xn = V(lambda e: e.tensor_scalar(xn_t, xc, rstd, None, ALU.mult), [tst, t_xndma])
        t_xndma = Y(lambda e, r0=r0, r1=r1: e.dma_start(scr["xn"][r0:r1, :], xn_t), [t_xn])
        t_tr = emit_transpose8(C, xn_t, std8(xnT), [t_xn, t_tr], bf16=False, pb_idx=0)
        tm = None
        for j in range(8):
            tm = T(lambda e, j=j: e.matmul(C.pb[2][:, 0:96], xnT[:, j * 128:(j + 1) * 128],
                                           Wx[:, j * 96:(j + 1) * 96],
                                           start=(j == 0), stop=(j == 7)),
                   [t_tr, tWx, t_pb2a_free] if j == 0 else [])
        t_xdbl = V(lambda e, lc=lc: e.tensor_copy(xdbl_sb[:, lc * 96:(lc + 1) * 96], C.pb[2][:, 0:96]), [tm])
        t_pb2a_free = t_xdbl
        Y(lambda e, lc=lc, r0=r0, r1=r1: e.dma_start(
            scr["xdbl"][r0:r1, :], xdbl_sb[:, lc * 96:(lc + 1) * 96]), [t_xdbl])
        ttr64 = T(lambda e, lc=lc: e.transpose(C.pb[2][0:64, 512:640],
                                               xdbl_sb[:, lc * 96:lc * 96 + 64], C.iden),
                  [t_xdbl, t_pb2b_free])
        tdT = V(lambda e: e.tensor_copy(dT, C.pb[2][0:64, 512:640]), [ttr64])
        t_pb2b_free = tdT
        tdm = None
        for h in range(2):
            tdm = T(lambda e, h=h: e.matmul(C.pb[1][:, h * 512:(h + 1) * 512], dT,
                                            Wdt[:, h * 512:(h + 1) * 512], start=True, stop=True),
                    [tdT, tWdt, t_pb1_free] if h == 0 else [])
        tz = V(lambda e: e.tensor_tensor(w_soft, C.pb[1], bdt_rep, ALU.add), [tdm, trep])
        t_pb1_free = tz
        tmx = V(lambda e: e.tensor_scalar(wa, w_soft, 0.0, None, ALU.max))
        tmn = V(lambda e: e.tensor_scalar(wb, w_soft, 0.0, None, ALU.min))
        S(lambda e: e.activation(wb, wb, AF.Exp), [tmn])
        te2 = S(lambda e: e.activation(wc, wa, AF.Exp, scale=-1.0), [tmx])
        tse = V(lambda e: e.tensor_tensor(wb, wb, wc, ALU.add), [te2])
        tl1 = S(lambda e: e.activation(wb, wb, AF.Ln), [tse])
        t_dl = V(lambda e: e.tensor_tensor(delta_t, wa, wb, ALU.add), [tl1])
        t_du = V(lambda e: e.tensor_tensor(w_du, delta_t, xn_t, ALU.mult), [t_dudma])
        t_dudma = G(lambda e, r0=r0, r1=r1: e.dma_start(scr["du"][r0:r1, :], w_du), [t_du])
        if lc > 0:
            t_car = S(lambda e: e.copy(pdcar, C.pb[3][0:1, :]), [t_colsum, t_pdmm])
        else:
            t_car = tzero
        tpd = None
        for h in range(2):
            T(lambda e, h=h: e.matmul(C.pb[1][:, h * 512:(h + 1) * 512], C.triu,
                                      delta_t[:, h * 512:(h + 1) * 512], start=True, stop=False),
              [t_dl, tz] if h == 0 else [])
            tpd = T(lambda e, h=h: e.matmul(C.pb[1][:, h * 512:(h + 1) * 512], C.onesr,
                                            pdcar[0:1, h * 512:(h + 1) * 512], start=False, stop=True),
                    [t_car])
        t_pdmm = tpd
        tcs = None
        for h in range(2):
            tcs = T(lambda e, h=h, lc=lc: e.matmul(C.pb[3][0:1, h * 512:(h + 1) * 512], C.onesc,
                                                   delta_t[:, h * 512:(h + 1) * 512],
                                                   start=(lc == 0), stop=(lc == NLC - 1)),
                    [t_car] if h == 0 else [])
        t_colsum = tcs
        t_pdevac = V(lambda e: e.tensor_copy(w_pd, C.pb[1]), [tpd, t_pddma])
        t_pb1_free = t_pdevac
        t_pddma = Y(lambda e, r0=r0, r1=r1: e.dma_start(scr["pd"][r0:r1, :], w_pd), [t_pdevac])
        tq = emit_quantize(C, xn_t, xd_bf, ws, [t_g0dma, t_dmp])
        if DBG_DUMPS:
            t_dmp = G(lambda e, r0=r0, r1=r1: e.dma_start(scr["xd"][r0:r1, :], xd_bf), [tq])
        t_xdT = emit_transpose8(C, xd_bf, std8(xdT_bf), [tq], bf16=True, pb_idx=0)
        tf = None
        for j in range(8):
            for h in range(2):
                tf = T(lambda e, j=j, h=h: e.matmul(
                    C.pb[1][:, h * 512:(h + 1) * 512], xdT_bf[:, j * 128:(j + 1) * 128],
                    wdT_bf[:, j * 1024 + h * 512: j * 1024 + (h + 1) * 512],
                    start=(j == 0), stop=(j == 7)),
                    [t_xdT, t_pdevac] if (j == 0 and h == 0) else [])
        tf0 = V(lambda e: e.tensor_tensor(wa, C.pb[1], fb_rep, ALU.add), [tf])
        t_pb1_free = tf0
        tg0 = S(lambda e: e.activation(wb, wa, AF.Silu), [tf0])
        t_g0dma = Y(lambda e, r0=r0, r1=r1: e.dma_start(scr["g0"][r0:r1, :], wb), [tg0])

    return [t_g0dma, t_pddma, t_dudma, t_xndma, t_colsum, t_xdbl]


def emit_phaseB(C, din, scr, waits0):
    P, ar = C.P, C.ar
    V = lambda fn, w=(), f=False: P.op("vector", fn, waits=w, fence=f)
    S = lambda fn, w=(): P.op("scalar", fn, waits=w)
    T = lambda fn, w=(): P.op("tensor", fn, waits=w)
    Y = lambda fn, w=(): P.op("sync", fn, waits=w, dma=True)
    G = lambda fn, w=(): P.op("gpsimd", fn, waits=w, dma=True)

    posA = ar.afp(16 * 1024)
    TdRep = ar.afp(1024); DvRep = ar.afp(1024); boutRep = ar.afp(1024)
    carr_sb = ar.afp(1024, parts=16)
    slot = [ar.afp(1024, parts=1), ar.afp(1024, parts=1)]
    arow = slot
    alog = ar.afp(128); alogT = ar.afp(128)
    bc_t = [ar.afp(32), ar.afp(32)]
    pd0 = ar.afp(1024); pd_t = [pd0, pd0]
    du_t = [ar.afp(1024), ar.afp(1024)]
    R_t = [ar.afp(1024), ar.afp(1024)]
    sp0 = ar.afp(1024); Sp_t = [sp0, sp0]
    E_t = [ar.afp(1024), ar.afp(1024)]
    q_t = [ar.afp(1024), ar.afp(1024)]
    r_t = [ar.afp(1024), ar.afp(1024)]
    xn_t = ar.afp(1024); x_t = ar.afp(1024); g0_t = ar.afp(1024)
    row1 = xn_t[0:1, :]
    y_acc = ar.afp(1024); zt = ar.afp(1024); s0_t = ar.afp(1024); x1_t = ar.afp(1024)
    xs_t = ar.afp(1024)
    wout_bf = ar.abf(8192); zb = ar.abf(1024); zT_bf = ar.abf(1024)

    t_wout = G(lambda e: e.dma_start(wout_bf.rearrange("p (t e) -> p t e", e=1024),
                                     din["L0_attn_W_out"].rearrange("(t p) e -> p t e", p=128)), waits0)
    reps = {}
    tprev = None
    for name, key, rep in [("td", None, TdRep), ("dv", "L0_attn_D", DvRep),
                           ("bo", "L0_attn_b_out", boutRep)]:
        if key is None:
            tl = Y(lambda e: e.dma_start(row1, scr["pd"][L - 1:L, :]), list(waits0))
        else:
            tl = Y(lambda e, key=key: e.dma_start(row1, din[key].rearrange("(o c) -> o c", o=1)),
                   [tprev])
        tvh = None
        for h in range(2):
            tm = T(lambda e, h=h: e.matmul(C.pb[0][:, h * 512:(h + 1) * 512], C.onesr,
                                           row1[:, h * 512:(h + 1) * 512], start=True, stop=True),
                   [tl] + list(waits0) + ([tvh] if tvh else []))
            tvh = V(lambda e, rep=rep, h=h: e.tensor_copy(rep[:, h * 512:(h + 1) * 512],
                                                          C.pb[0][:, h * 512:(h + 1) * 512]), [tm])
        tprev = tvh
        reps[name] = tvh
    tla = Y(lambda e: e.dma_start(alog.rearrange("p (t n) -> p t n", n=16),
                                  din["L0_attn_A_log"].rearrange("(t p) n -> p t n", p=128)), waits0)
    tta = T(lambda e: e.transpose(C.pb[0][:, 0:128], alog, C.iden), [tla, tprev])
    tva = V(lambda e: e.tensor_copy(alogT, C.pb[0][:, 0:128]), [tta])
    tga = G(lambda e: e.dma_start(scr["alogt"], alogT), [tva])
    t_posA = []
    tprevA = [tva, tva]
    for n in range(16):
        p2 = n % 2
        tl = Y(lambda e, n=n, p2=p2: e.dma_start(
            arow[p2].rearrange("o (t p) -> o t p", p=128),
            scr["alogt"].rearrange("(t n) p -> n t p", n=16)[n:n + 1]), [tga, tprevA[p2]])
        tvh = None
        for h in range(2):
            tm = T(lambda e, h=h, p2=p2: e.matmul(C.pb[0][:, h * 512:(h + 1) * 512], C.onesr,
                                                  arow[p2][:, h * 512:(h + 1) * 512],
                                                  start=True, stop=True), [tl] + ([tvh] if tvh else []))
            tvh = S(lambda e, n=n, h=h: e.activation(
                posA[:, n * 1024 + h * 512: n * 1024 + (h + 1) * 512],
                C.pb[0][:, h * 512:(h + 1) * 512], AF.Exp), [tm])
        tprevA[p2] = tvh
        t_posA.append(tvh)
    tz0 = V(lambda e: e.memset(carr_sb, 0.0), waits0)
    t_car0 = G(lambda e: e.dma_start(scr["carr"], carr_sb), [tz0])

    numbank = [C.pb[1], C.pb[2]]
    tk = {nm: {} for nm in ["Sp", "E", "q", "num", "cols", "xs", "slot", "r"]}
    g = lambda nm, k: tk[nm].get(k)
    t_scar = {-1: t_car0}
    t_ldpd = {}; t_lddu = {}; t_ldbc = {}
    t_R = {}
    t_y = None; t_z2 = None; t_s0evac = {}; t_x1dma = None
    t_eld = {"xn": reps["bo"], "x": None, "g0": None}
    t_tr = t_wout

    def qlast(lc):
        return g("q", lc * 16 + 15)

    def splast(lc):
        return g("Sp", lc * 16 + 15)

    def lds(lc):
        if lc >= NLC:
            return
        r0, r1 = lc * 128, (lc + 1) * 128
        p = lc % 2
        t_ldpd[lc] = Y(lambda e, p=p, r0=r0, r1=r1: e.dma_start(pd_t[p], scr["pd"][r0:r1, :]),
                       [t_R.get(lc - 1), splast(lc - 1)] + (list(waits0) if lc < 2 else []))
        t_lddu[lc] = Y(lambda e, p=p, r0=r0, r1=r1: e.dma_start(du_t[p], scr["du"][r0:r1, :]),
                       [qlast(lc - 2)])
        t_ldbc[lc] = Y(lambda e, p=p, r0=r0, r1=r1: e.dma_start(bc_t[p], scr["xdbl"][r0:r1, 64:96]),
                       [qlast(lc - 2), t_y if lc >= 2 else None])

    lds(0); lds(1)
    for lc in range(NLC):
        p = lc % 2
        r0, r1 = lc * 128, (lc + 1) * 128
        t_R[lc] = V(lambda e, p=p: e.tensor_tensor(R_t[p], TdRep, pd_t[p], ALU.subtract),
                    [t_ldpd[lc], reps["td"]])
        if lc > 0:
            t_cpy = S(lambda e: e.copy(carr_sb, C.pb[3][0:16, :]),
                      [g("cols", lc * 16 - 1), t_scar[lc - 1]])
            t_scar[lc] = G(lambda e: e.dma_start(scr["carr"], carr_sb), [t_cpy])
        else:
            t_scar[0] = t_car0
        def emit_xs_y(k2, p2):
            n2 = k2 % 16
            np2 = n2 % 2
            bank2 = numbank[np2]
            tk["xs"][k2] = V(lambda e, np_=np2, bank=bank2: e.tensor_tensor(
                xs_t, bank, r_t[np_], ALU.mult), [tk["num"][k2], tk["r"][k2]])
            nonlocal t_y
            if n2 == 0:
                t_y = V(lambda e, p=p2: e.tensor_scalar(y_acc, xs_t, bc_t[p][:, 16:17], None,
                                                        ALU.mult), [t_y, t_z2])
            else:
                t_y = V(lambda e, n=n2, p=p2: e.scalar_tensor_tensor(
                    y_acc, xs_t, bc_t[p][:, 16 + n:16 + n + 1], y_acc, ALU.mult, ALU.add))

        for n in range(16):
            k = lc * 16 + n
            np_ = n % 2
            tk["slot"][k] = G(lambda e, n=n, np_=np_: e.dma_start(slot[np_], scr["carr"][n:n + 1, :]),
                              [t_scar[lc], g("num", k - 2)] + ([t_posA[15]] if k < 2 else []))
            tk["Sp"][k] = V(lambda e, n=n, np_=np_, p=p: e.tensor_tensor(
                Sp_t[np_], posA[:, n * 1024:(n + 1) * 1024], R_t[p], ALU.mult),
                [t_R[lc], t_posA[n], g("E", k - 1)])
            tk["E"][k] = S(lambda e, np_=np_: e.activation(E_t[np_], Sp_t[np_], AF.Exp, scale=-1.0),
                           [tk["Sp"][k], g("q", k - 2)])
            tk["q"][k] = V(lambda e, np_=np_, p=p, n=n: e.scalar_tensor_tensor(
                q_t[np_], du_t[p], bc_t[p][:, n:n + 1], E_t[np_], ALU.mult, ALU.mult),
                [tk["E"][k], t_lddu[lc], t_ldbc[lc], g("cols", k - 2)])
            trl = S(lambda e, np_=np_: e.activation(r_t[np_], E_t[np_], AF.Ln, bias=C.ccol[:, 1:2]),
                    [g("xs", k - 2)])
            tk["r"][k] = P.op("scalar", lambda e, np_=np_: e.activation(r_t[np_], r_t[np_], AF.Exp,
                                                                        scale=-1.0),
                              waits=[trl], fence=True)
            bank = numbank[np_]
            tn = None
            w0 = [tk["q"][k], g("xs", k - 2)]
            if np_ == 1:
                w0.append(t_s0evac.get(lc - 1))
            for h in range(2):
                T(lambda e, h=h, np_=np_, bank=bank: e.matmul(
                    bank[:, h * 512:(h + 1) * 512], C.triu, q_t[np_][:, h * 512:(h + 1) * 512],
                    start=True, stop=False), w0 if h == 0 else [])
                tn = T(lambda e, h=h, np_=np_: e.matmul(
                    numbank[np_][:, h * 512:(h + 1) * 512], C.onesr,
                    slot[np_][0:1, h * 512:(h + 1) * 512], start=False, stop=True),
                    [tk["slot"][k]])
            tk["num"][k] = tn
            tc = None
            for h in range(2):
                tc = T(lambda e, h=h, n=n, np_=np_, lc=lc: e.matmul(
                    C.pb[3][0:16, h * 512:(h + 1) * 512], C.indic[:, n * 16:(n + 1) * 16],
                    q_t[np_][:, h * 512:(h + 1) * 512],
                    start=(lc == 0 and n == 0), stop=(lc == NLC - 1 and n == 15)),
                    [t_scar[lc]] if (n == 0 and h == 0) else [])
            tk["cols"][k] = tc
            if n > 0:
                emit_xs_y(k - 1, p)
        emit_xs_y(lc * 16 + 15, p)
        t_ep = {}
        for nm, buf in [("xn", xn_t), ("x", x_t), ("g0", g0_t)]:
            srcd = din["x"] if nm == "x" else scr[nm]
            t_ep[nm] = Y(lambda e, srcd=srcd, buf=buf, r0=r0, r1=r1: e.dma_start(
                buf, srcd[r0:r1, :]), [t_eld[nm]])
        tz1 = V(lambda e: e.tensor_tensor(zt, xn_t, DvRep, ALU.mult),
                [t_ep["xn"], reps["dv"], t_y])
        t_z2 = V(lambda e: e.tensor_tensor(zt, zt, y_acc, ALU.add))
        t_eld["xn"] = t_z2
        tzb = V(lambda e: e.tensor_copy(zb, zt))
        ttr = emit_transpose8(C, zb, std8(zT_bf), [tzb, t_tr], bf16=True, pb_idx=0)
        t_tr = ttr
        tm = None
        for j in range(8):
            for h in range(2):
                tm = T(lambda e, j=j, h=h: e.matmul(
                    C.pb[2][:, h * 512:(h + 1) * 512], zT_bf[:, j * 128:(j + 1) * 128],
                    wout_bf[:, j * 1024 + h * 512: j * 1024 + (h + 1) * 512],
                    start=(j == 0), stop=(j == 7)),
                    [ttr, t_wout, g("xs", lc * 16 + 15)] if (j == 0 and h == 0) else [])
        ts0 = V(lambda e: e.tensor_tensor(s0_t, C.pb[2], boutRep, ALU.add), [tm, reps["bo"]])
        t_s0evac[lc] = ts0
        tm1 = V(lambda e: e.tensor_tensor(s0_t, s0_t, g0_t, ALU.mult), [t_ep["g0"]])
        tx1 = V(lambda e: e.tensor_tensor(x1_t, x_t, s0_t, ALU.add), [t_ep["x"], t_x1dma])
        t_eld["g0"] = tx1; t_eld["x"] = tx1
        t_x1dma = Y(lambda e, r0=r0, r1=r1: e.dma_start(scr["x1"][r0:r1, :], x1_t), [tx1])
        lds(lc + 2)

    return [t_x1dma, t_s0evac[NLC - 1], tk["cols"][NLC * 16 - 1], t_scar[NLC - 1]]


def emit_phaseC(C, din, scr, waits0):
    P, ar = C.P, C.ar
    V = lambda fn, w=(), f=False: P.op("vector", fn, waits=w, fence=f)
    S = lambda fn, w=(): P.op("scalar", fn, waits=w)
    T = lambda fn, w=(): P.op("tensor", fn, waits=w)
    Y = lambda fn, w=(): P.op("sync", fn, waits=w, dma=True)
    G = lambda fn, w=(): P.op("gpsimd", fn, waits=w, dma=True)

    x1_t = [ar.afp(1024), ar.afp(1024)]
    xc = ar.afp(1024); xn1 = ar.afp(1024); nt = ar.afp(1024)
    xnTf = ar.afp(1024); ntTf = ar.afp(1024)
    w1 = ar.afp(1024)
    nwRep = ar.afp(1024)
    rw_sb = ar.afp(1024, parts=8)
    rwT = ar.afp(64)
    mcol = ar.afp(1); mcol2 = ar.afp(1); vcol = ar.afp(1); rstd = ar.afp(1)
    pr = ar.afp(8); m1 = ar.afp(1); m2 = ar.afp(1); eq1 = ar.afp(8); eq2 = ar.afp(8)
    pm = ar.afp(8); dcol = ar.afp(1); drc = ar.afp(1); comb_t = ar.afp(8)
    xnT_bf = ar.abf(1024); ntT_bf = ar.abf(1024)

    hview = lambda t, lc: scr[t].rearrange("p (t l) -> p t l", l=L)[:, :, lc * 128:(lc + 1) * 128]

    trow = Y(lambda e: e.dma_start(w1[0:1, :], din["L1_ffn_norm_weight"].rearrange("(o c) -> o c", o=1)),
             waits0)
    tvh = None
    for h in range(2):
        tm = T(lambda e, h=h: e.matmul(C.pb[0][:, h * 512:(h + 1) * 512], C.onesr,
                                       w1[0:1, h * 512:(h + 1) * 512], start=True, stop=True),
               [trow] + list(waits0) + ([tvh] if tvh else []))
        tvh = V(lambda e, h=h: e.tensor_copy(nwRep[:, h * 512:(h + 1) * 512],
                                             C.pb[0][:, h * 512:(h + 1) * 512]), [tm])
    t_nw = tvh
    trw = Y(lambda e: e.dma_start(rw_sb, din["L1_ffn_router_w"]), waits0)
    t_rwT = None
    for j in range(8):
        hh = j % 2
        tt = T(lambda e, j=j, hh=hh: e.transpose(C.pb[1][0:128, hh * 512:hh * 512 + 8],
                                                 rw_sb[0:8, j * 128:(j + 1) * 128], C.iden[0:8, 0:8]),
               [trw, t_nw] + ([t_rwT] if t_rwT else []))
        t_rwT = V(lambda e, j=j, hh=hh: e.tensor_copy(rwT[:, j * 8:(j + 1) * 8],
                                                      C.pb[1][0:128, hh * 512:hh * 512 + 8]), [tt])

    t_ldx = {}
    t_con = [None, None]
    t_hd = {"xn": None, "nt": None}
    t_tr = t_rwT
    t_comb = None
    t_prdone = None

    def ld(lc):
        if lc >= NLC:
            return
        p = lc % 2
        t_ldx[lc] = Y(lambda e, p=p, lc=lc: e.dma_start(x1_t[p], scr["x1"][lc * 128:(lc + 1) * 128, :]),
                      [t_con[p]])

    ld(0); ld(1)
    for lc in range(NLC):
        p = lc % 2
        t1 = V(lambda e, p=p: e.reduce_sum(mcol, x1_t[p], axis=AX.X), [t_ldx[lc]])
        tms = S(lambda e: e.mul(mcol2, mcol, 1.0 / 1024.0), [t1])
        t_xc = V(lambda e, p=p: e.tensor_scalar(xc, x1_t[p], mcol2, None, ALU.subtract), [tms])
        t_con[p] = t_xc
        V(lambda e: e.tensor_tensor(w1, xc, xc, ALU.mult))
        tvs = V(lambda e: e.reduce_sum(vcol, w1, axis=AX.X))
        tl1 = S(lambda e: e.activation(rstd, vcol, AF.Ln, bias=C.ccol[:, 0:1], scale=1.0 / 1024.0), [tvs])
        tst = P.op("scalar", lambda e: e.activation(rstd, rstd, AF.Exp, scale=-0.5),
                   waits=[tl1], fence=True)
        t_xn1 = V(lambda e: e.tensor_scalar(xn1, xc, rstd, None, ALU.mult), [tst, t_hd["xn"]])
        V(lambda e: e.tensor_tensor(w1, xn1, xn1, ALU.mult))
        tvs2 = V(lambda e: e.reduce_sum(vcol, w1, axis=AX.X))
        tl2 = S(lambda e: e.activation(rstd, vcol, AF.Ln, bias=C.ccol[:, 3:4], scale=1.0 / 1024.0), [tvs2])
        tst2 = P.op("scalar", lambda e: e.activation(rstd, rstd, AF.Exp, scale=-0.5),
                    waits=[tl2], fence=True)
        tn1 = V(lambda e: e.tensor_scalar(nt, xn1, rstd, None, ALU.mult), [tst2, t_hd["nt"]])
        tn2 = V(lambda e: e.tensor_tensor(nt, nt, nwRep, ALU.mult), [t_nw])
        ttr1 = emit_transpose8(C, xn1, std8(xnTf), [t_xn1, t_tr], bf16=False, pb_idx=0)
        tc1 = V(lambda e: e.tensor_copy(xnT_bf, xnTf), [ttr1])
        t_hd["xn"] = G(lambda e, lc=lc: e.dma_start(
            hview("xn1t_hbm", lc), xnT_bf.rearrange("p (t l) -> p t l", l=128)), [tc1])
        ttr2 = emit_transpose8(C, nt, std8(ntTf), [tn2, tc1, t_prdone], bf16=False, pb_idx=0)
        t_tr = ttr2
        tc2 = V(lambda e: e.tensor_copy(ntT_bf, ntTf), [ttr2])
        t_hd["nt"] = G(lambda e, lc=lc: e.dma_start(
            hview("normt_hbm", lc), ntT_bf.rearrange("p (t l) -> p t l", l=128)), [tc2])
        tmm = None
        for j in range(8):
            tmm = T(lambda e, j=j: e.matmul(C.pb[2][:, 0:8], ntTf[:, j * 128:(j + 1) * 128],
                                            rwT[:, j * 8:(j + 1) * 8], start=(j == 0), stop=(j == 7)),
                    [ttr2, t_rwT, t_prdone] if j == 0 else [])
        tpr = S(lambda e: e.activation(pr, C.pb[2][:, 0:8], AF.Sigmoid), [tmm])
        t_prdone = tpr
        tA = V(lambda e: e.reduce_max(m1, pr, axis=AX.X), [tpr])
        tB = P.op("vector", lambda e: e.tensor_scalar(eq1, pr, m1, None, ALU.is_equal),
                  waits=[tA], fence=True)
        tC = P.op("vector", lambda e: e.scalar_tensor_tensor(pm, eq1, -1e30, pr, ALU.mult, ALU.add),
                  waits=[tB], fence=True)
        tD = P.op("vector", lambda e: e.reduce_max(m2, pm, axis=AX.X), waits=[tC], fence=True)
        tE = P.op("vector", lambda e: e.tensor_scalar(eq2, pm, m2, None, ALU.is_equal),
                  waits=[tD], fence=True)
        tF = P.op("vector", lambda e: e.tensor_tensor(dcol, m1, m2, ALU.add), waits=[tD], fence=True)
        tG2 = P.op("vector", lambda e: e.tensor_scalar(dcol, dcol, 1e-6, None, ALU.add),
                   waits=[tF], fence=True)
        tH = S(lambda e: e.activation(drc, dcol, AF.Ln), [tG2])
        tI = P.op("scalar", lambda e: e.activation(drc, drc, AF.Exp, scale=-1.0),
                  waits=[tH], fence=True)
        tJ = V(lambda e: e.tensor_tensor(w1[:, 0:1], m1, drc, ALU.mult), [tI])
        tK = V(lambda e: e.tensor_tensor(w1[:, 1:2], m2, drc, ALU.mult))
        tL = P.op("vector", lambda e: e.tensor_scalar(comb_t, eq1, w1[:, 0:1], None, ALU.mult),
                  waits=[tK, tE, t_comb], fence=True)
        tM = P.op("vector", lambda e: e.scalar_tensor_tensor(comb_t, eq2, w1[:, 1:2], comb_t,
                                                             ALU.mult, ALU.add), waits=[tL], fence=True)
        t_comb = Y(lambda e, lc=lc: e.dma_start(scr["comb"][lc * 128:(lc + 1) * 128, :], comb_t), [tM])
        ld(lc + 2)

    return [t_comb, t_hd["xn"], t_hd["nt"]]


def emit_phaseC2(C, din, scr, waits0):
    P, ar = C.P, C.ar
    V = lambda fn, w=(), f=False: P.op("vector", fn, waits=w, fence=f)
    S = lambda fn, w=(): P.op("scalar", fn, waits=w)
    T = lambda fn, w=(): P.op("tensor", fn, waits=w)
    Y = lambda fn, w=(): P.op("sync", fn, waits=w, dma=True)
    G = lambda fn, w=(): P.op("gpsimd", fn, waits=w, dma=True)

    xn1T = ar.abf(8 * 4096)
    gatT = ar.abf(8 * 4096)
    wo_bf = ar.abf(8192)
    wq_bf = ar.abf(512); wk_bf = ar.abf(512); wv_bf = ar.abf(512); wg_bf = ar.abf(512)
    _qv = ar.abf(4096); qT = _qv[0:64, :]; vT = _qv[64:128, :]
    _kg = ar.abf(4096); kT = _kg[0:64, :]; gT = _kg[64:128, :]
    Vb = [ar.abf(66, parts=128), ar.abf(66, parts=128)]
    Kb = [ar.abf(64, parts=128), ar.abf(64, parts=128)]
    Sb = ar.abf(66, parts=64)
    Am = ar.abf(128)
    tq1 = ar.afp(512, parts=64); tq2 = ar.afp(512, parts=64)
    _ng = ar.afp(128); nm_t = _ng[0:64, :]; gt1 = _ng[64:128, :]
    dn_t = ar.afp(128, parts=1); dr_t = ar.afp(128, parts=1)
    s1_t = ar.afp(1024)
    onescol_bf = ar.abf(2)

    t_ld0 = G(lambda e: e.dma_start(xn1T, scr["xn1t_hbm"]), waits0)
    t_wo = G(lambda e: e.dma_start(wo_bf.rearrange("p (t e) -> p t e", e=1024),
                                   din["L1_attn_w_out"].rearrange("(t p) e -> p t e", p=128)),
             waits0)
    tob = V(lambda e: e.memset(onescol_bf, 1.0), waits0)
    tv0 = V(lambda e: e.tensor_copy(Vb[0][:, 64:65], onescol_bf[:, 0:1]), [tob])
    tv1 = V(lambda e: e.tensor_copy(Vb[1][:, 64:65], onescol_bf[:, 0:1]))

    t_wld = [None] * 4
    t_qkvg = {}
    t_h = {}
    t_am = None; t_nm = None; t_gat = None
    t_vtr = [tv0, tv1]
    t_ktr = [None, None]
    for h in range(C2_HEADS):
        if C2_HEADBAR and h > 0:
            P.barrier([t_am, t_nm, t_gat, t_h.get(h - 1)])
        offs = [h * 64, 1024 + h * 64, 2048 + h * 64]
        dsts = [wq_bf, wk_bf, wv_bf]
        tl = []
        for i in range(3):
            tl.append(G(lambda e, i=i, o=offs[i]: e.dma_start(
                dsts[i].rearrange("p (t e) -> p t e", e=64),
                din["L1_attn_w_qkv"][:, o:o + 64].rearrange("(t p) e -> p t e", p=128)),
                [t_wld[i]]))
        tl.append(G(lambda e, h=h: e.dma_start(
            wg_bf.rearrange("p (t e) -> p t e", e=64),
            din["L1_attn_w_gate"][:, h * 64:h * 64 + 64].rearrange("(t p) e -> p t e", p=128)),
            [t_wld[3]]))
        tend = {}
        tbank = [t_am, t_nm]
        for ti, (wsb, dst) in enumerate([(wq_bf, qT), (wk_bf, kT), (wv_bf, vT), (wg_bf, gT)]):
            tprev = None
            for lg in range(8):
                tmm = None
                for j in range(8):
                    bank = C.pb[2][0:64, (lg % 2) * 512:(lg % 2 + 1) * 512]
                    tmm = T(lambda e, j=j, lg=lg, wsb=wsb, bank=bank: e.matmul(
                        bank, wsb[:, j * 64:(j + 1) * 64],
                        xn1T[:, j * 4096 + lg * 512: j * 4096 + (lg + 1) * 512],
                        start=(j == 0), stop=(j == 7)),
                        ([tl[ti], t_ld0, t_qkvg.get((h - 1, ti, lg)), tbank[lg % 2]]
                         if j == 0 else []))
                bank = C.pb[2][0:64, (lg % 2) * 512:(lg % 2 + 1) * 512]
                sl = slice(lg * 512, (lg + 1) * 512)
                if ti < 2:
                    ta = V(lambda e, bank=bank: e.tensor_scalar(tq1, bank, 0.0, None, ALU.min), [tmm])
                    tb = S(lambda e: e.activation(tq1, tq1, AF.Exp), [ta])
                    tc = V(lambda e, bank=bank: e.tensor_scalar(tq2, bank, 0.0, None, ALU.max), [tb])
                    tev = V(lambda e, dst=dst, sl=sl: e.tensor_tensor(dst[:, sl], tq1, tq2, ALU.add))
                elif ti == 3:
                    tev = S(lambda e, dst=dst, sl=sl, bank=bank: e.activation(dst[:, sl], bank,
                                                                             AF.Sigmoid), [tmm])
                else:
                    tev = V(lambda e, dst=dst, sl=sl, bank=bank: e.tensor_copy(dst[:, sl], bank), [tmm])
                t_qkvg[(h, ti, lg)] = tev
                tbank[lg % 2] = tev
                tprev = tev
                if lg == 7:
                    t_wld[ti] = tmm
            tend[ti] = tprev
        if h == 0 and DBG_DUMPS:
            G(lambda e: e.dma_start(scr["qt0_hbm"], qT), [tend[0]])
        tsz = V(lambda e: e.memset(Sb, 0.0), [t_h.get(h - 1)])
        t_sup = None
        for lc in range(NLC):
            cp = lc % 2
            sl = slice(lc * 128, (lc + 1) * 128)
            tt1 = T(lambda e, sl=sl, cp=cp: e.transpose(C.pbf(0, cp)[:, 0:64], vT[:, sl],
                                                        C.idenb[64:128, 64:128]),
                    [tend[2], t_vtr[cp]])
            V(lambda e, cp=cp: e.tensor_copy(Vb[cp][:, 0:64], C.pbf(0, cp)[:, 0:64]), [tt1])
            tvb = P.op("vector", lambda e, cp=cp: e.tensor_copy(Vb[cp][:, 64:65],
                                                                onescol_bf[:, 0:1]),
                       waits=[tob], fence=True)
            tt2 = T(lambda e, sl=sl, cp=cp: e.transpose(C.pbf(1, cp)[:, 0:64], kT[:, sl],
                                                        C.idenb[0:64, 0:64]),
                    [tend[1], t_ktr[cp]])
            tkb = V(lambda e, cp=cp: e.tensor_copy(Kb[cp], C.pbf(1, cp)[:, 0:64]), [tt2])
            tam = T(lambda e, sl=sl: e.matmul(C.pb[2][:, 0:128], kT[:, sl], qT[:, sl],
                                              start=True, stop=True),
                    [tend[0], tend[1], t_am] + ([tend[3]] if lc == 0 else []))
            tms = V(lambda e: e.tensor_tensor(Am, C.pb[2][:, 0:128], C.triu, ALU.mult), [tam])
            t_am = tms
            if DBG_DUMPS and h == 0:
                tdmp = G(lambda e, lc=lc: e.dma_start(
                    scr["am_hbm"][:, lc * 128:(lc + 1) * 128], Am), [tms])
                tms = tdmp
            tn1 = T(lambda e, cp=cp, lc=lc: e.matmul(C.pb[2][0:65, 512:640], Vb[cp][:, 0:65], Am,
                                                     start=True, stop=(lc == 0)),
                    [tms, tvb, t_nm])
            if lc > 0:
                tn1 = T(lambda e, sl=sl: e.matmul(C.pb[2][0:65, 512:640], Sb[:, 0:65], qT[:, sl],
                                                  start=False, stop=True), [t_sup])
            td1 = V(lambda e: e.tensor_scalar(dn_t, C.pb[2][64:65, 512:640], 1e-6, None, ALU.add),
                    [tn1])
            if DBG_DUMPS and h == 0:
                tdd = Y(lambda e, lc=lc: e.dma_start(scr["den"][lc:lc + 1, :], dn_t), [td1])
                td1 = tdd
            td2 = S(lambda e: e.activation(dr_t, dn_t, AF.Ln), [td1])
            td3 = P.op("scalar", lambda e: e.activation(dr_t, dr_t, AF.Exp, scale=-1.0),
                       waits=[td2], fence=True)
            tbc = T(lambda e: e.matmul(C.pb[3][0:64, 0:128], C.onesr[0:1, 0:64], dr_t,
                                       start=True, stop=True), [td3, t_gat])
            tnm = V(lambda e: e.tensor_copy(nm_t, C.pb[2][0:64, 512:640]), [tn1])
            t_nm = tnm
            tg1 = V(lambda e: e.tensor_tensor(gt1, nm_t, C.pb[3][0:64, 0:128], ALU.mult), [tbc])
            t_gat = tg1
            hrow = (h % 2) * 64
            hcol = (h // 2) * 4096
            tg2 = V(lambda e, sl=sl, hrow=hrow, hcol=hcol, lc=lc: e.tensor_tensor(
                gatT[hrow:hrow + 64, hcol + lc * 128: hcol + (lc + 1) * 128], gt1,
                gT[:, sl], ALU.mult), [tend[3]])
            tsu = T(lambda e, cp=cp, lc=lc: e.matmul(C.pb[3][0:64, 640:705], Kb[cp],
                                                     Vb[cp][:, 0:65],
                                                     start=(lc == 0), stop=(lc == NLC - 1)),
                    [tkb, tvb, tsz] + ([t_sup] if t_sup else []))
            tsb = V(lambda e: e.tensor_copy(Sb[:, 0:65], C.pb[3][0:64, 640:705]), [tsu])
            t_sup = tsb
            t_vtr[cp] = tvb
            t_ktr[cp] = tkb
        t_h[h] = tg2
    if DBG_DUMPS:
        t_all0 = [t_h[hh] for hh in range(C2_HEADS)]
        G(lambda e: e.dma_start(scr["gatt_hbm"], gatT), t_all0)
        G(lambda e: e.dma_start(scr["qt_hbm"], qT), t_all0)
        G(lambda e: e.dma_start(scr["kt_hbm"], kT), t_all0)
        G(lambda e: e.dma_start(scr["vt_hbm"], vT), t_all0)
        G(lambda e: e.dma_start(scr["gt_hbm"], gT), t_all0)
    t_s1 = None
    tails = []
    t_all = [t_h[hh] for hh in range(C2_HEADS)]
    for lc in range(NLC):
        tmm = None
        for j in range(8):
            for hh2 in range(2):
                tmm = T(lambda e, j=j, hh2=hh2, lc=lc: e.matmul(
                    C.pb[2][:, hh2 * 512:(hh2 + 1) * 512], gatT[:, j * 4096 + lc * 128:j * 4096 + (lc + 1) * 128],
                    wo_bf[:, j * 1024 + hh2 * 512: j * 1024 + (hh2 + 1) * 512],
                    start=(j == 0), stop=(j == 7)),
                    (t_all + [t_wo, t_s1]) if (j == 0 and hh2 == 0) else [])
        tev = V(lambda e: e.tensor_copy(s1_t, C.pb[2]), [tmm])
        t_s1 = Y(lambda e, lc=lc: e.dma_start(scr["s1"][lc * 128:(lc + 1) * 128, :], s1_t), [tev])
        tails = [t_s1]
    return tails


def emit_phaseD(C, din, scr, out_d, waits0):
    P, ar = C.P, C.ar
    V = lambda fn, w=(), f=False: P.op("vector", fn, waits=w, fence=f)
    S = lambda fn, w=(): P.op("scalar", fn, waits=w)
    T = lambda fn, w=(): P.op("tensor", fn, waits=w)
    Y = lambda fn, w=(): P.op("sync", fn, waits=w, dma=True)
    G = lambda fn, w=(): P.op("gpsimd", fn, waits=w, dma=True)

    ew1_sb = ar.abf(8 * 4096)
    ew2_sb = ar.abf(32 * 1024)
    h_buf = ar.abf(32 * 512)
    normT = [ar.abf(8 * 512), ar.abf(8 * 512)]
    comb_res = ar.afp(256)
    b1col = ar.afp(32)
    b1stg = ar.afp(128, parts=32)
    b2rep = ar.afp(1024)
    wv = [ar.afp(1024), ar.afp(1024)]
    acc_t = h_buf[:, 0:2048].bitcast(F32)
    x1_t = h_buf[:, 2048:4096].bitcast(F32)
    s1_t = h_buf[:, 4096:6144].bitcast(F32)
    row1 = b2rep[0:1, :]

    tcomb = Y(lambda e: e.dma_start(comb_res.rearrange("p (t e) -> p t e", e=8),
                                    scr["comb"].rearrange("(t p) e -> p t e", p=128)), waits0)

    t_ew1last = None; t_ew2last = None
    t_b1 = None; t_b2user = None
    t_nld = [None, None]
    t_relu = {}
    t_acc = {}
    t_wvdma = [None, None]
    t_ps = {0: None, 1: None}
    t_hb = {0: None, 1: None}
    t_mm2l = None
    gls = 0

    for e in range(9):
        if e == 0:
            w1src = din["L1_ffn_sw1"]; w2src = din["L1_ffn_sw2"]
            b1src = din["L1_ffn_sb1"]; b2src = din["L1_ffn_sb2"]
        else:
            w1src = din["L1_ffn_ew1"][e - 1]; w2src = din["L1_ffn_ew2"][e - 1]
            b1src = din["L1_ffn_eb1"][e - 1]; b2src = din["L1_ffn_eb2"][e - 1]
        tw1 = G(lambda e2, w1src=w1src: e2.dma_start(
            ew1_sb.rearrange("p (t h) -> p t h", h=4096),
            w1src.rearrange("(t p) h -> p t h", p=128)), [t_ew1last] + list(waits0))
        tw2 = G(lambda e2, w2src=w2src: e2.dma_start(
            ew2_sb.rearrange("p (t o) -> p t o", o=1024),
            w2src.rearrange("(t p) o -> p t o", p=128)), [t_ew2last])
        tb1s = Y(lambda e2, b1src=b1src: e2.dma_start(
            b1stg, b1src.rearrange("(c p) -> c p", p=128)), [t_b1])
        tb1t = T(lambda e2: e2.transpose(C.pb[1][:, 0:32], b1stg, C.iden[0:32, 0:32]),
                 [tb1s, t_b2user])
        tb1 = V(lambda e2: e2.tensor_copy(b1col, C.pb[1][:, 0:32]), [tb1t])
        tb2r = Y(lambda e2, b2src=b2src: e2.dma_start(
            row1, b2src.rearrange("(o c) -> o c", o=1)), [t_b2user])
        tvh = None
        for hh in range(2):
            tmb = T(lambda e2, hh=hh: e2.matmul(C.pb[1][:, hh * 512:(hh + 1) * 512], C.onesr,
                                                row1[0:1, hh * 512:(hh + 1) * 512],
                                                start=True, stop=True),
                    [tb2r, t_b2user, tb1] + ([tvh] if tvh else []))
            tvh = V(lambda e2, hh=hh: e2.tensor_copy(b2rep[:, hh * 512:(hh + 1) * 512],
                                                     C.pb[1][:, hh * 512:(hh + 1) * 512]), [tmb])
        t_b2 = tvh
        for lg in range(8):
            lp = lg % 2
            tnl = Y(lambda e2, lg=lg, lp=lp: e2.dma_start(
                normT[lp].rearrange("p (t l) -> p t l", l=512),
                scr["normt_hbm"].rearrange("p (t l) -> p t l", l=L)[:, :, lg * 512:(lg + 1) * 512]),
                [t_nld[lp]])
            for ht in range(32):
                hb = ht % 2
                bank = C.pb[0][:, hb * 512:(hb + 1) * 512]
                tm = None
                for dt in range(8):
                    tm = T(lambda e2, dt=dt, ht=ht, lp=lp, bank=bank: e2.matmul(
                        bank, ew1_sb[:, dt * 4096 + ht * 128: dt * 4096 + (ht + 1) * 128],
                        normT[lp][:, dt * 512:(dt + 1) * 512],
                        start=(dt == 0), stop=(dt == 7)),
                        [tnl, tw1, t_hb[hb]] if dt == 0 else [])
                t_relu[ht] = S(lambda e2, ht=ht, bank=bank: e2.activation(
                    h_buf[:, ht * 512:(ht + 1) * 512], bank, AF.Relu,
                    bias=b1col[:, ht:ht + 1]), [tm, tb1])
                t_hb[hb] = t_relu[ht]
                if lg == 7 and ht == 31:
                    t_ew1last = tm
            t_rl = t_relu[31]
            for ls in range(4):
                pp = gls % 2
                psb = C.pb[2] if pp == 0 else C.pb[3]
                tm = None
                for oh in range(2):
                    for ht in range(32):
                        tm = T(lambda e2, ht=ht, ls=ls, oh=oh, psb=psb: e2.matmul(
                            psb[:, oh * 512:(oh + 1) * 512],
                            h_buf[:, ht * 512 + ls * 128: ht * 512 + (ls + 1) * 128],
                            ew2_sb[:, ht * 1024 + oh * 512: ht * 1024 + (oh + 1) * 512],
                            start=(ht == 0), stop=(ht == 31)),
                            [t_rl, tw2, t_ps[pp]] if (ht == 0 and oh == 0) else [])
                t_mm2l = tm
                r0 = lg * 512 + ls * 128
                wvb = wv[gls % 2]
                if e == 0:
                    tev = V(lambda e2, psb=psb, wvb=wvb: e2.tensor_tensor(
                        wvb, psb, b2rep, ALU.add), [tm, t_b2, t_wvdma[gls % 2]])
                    t_ps[pp] = tev
                    t_acc[(lg, ls)] = G(lambda e2, r0=r0, wvb=wvb: e2.dma_start(
                        scr["acc"][r0:r0 + 128, :], wvb), [tev])
                else:
                    cidx = (lg * 4 + ls) * 8 + (e - 1)
                    te1 = V(lambda e2, psb=psb, wvb=wvb, cidx=cidx: e2.tensor_scalar(
                        wvb, psb, comb_res[:, cidx:cidx + 1], None, ALU.mult),
                        [tm, tcomb, t_wvdma[gls % 2]])
                    tev = V(lambda e2, wvb=wvb, cidx=cidx: e2.scalar_tensor_tensor(
                        wvb, b2rep, comb_res[:, cidx:cidx + 1], wvb, ALU.mult, ALU.add),
                        [t_b2])
                    t_ps[pp] = tev
                    t_acc[(lg, ls)] = G(lambda e2, r0=r0, wvb=wvb: e2.dma_start(
                        scr["acc"][r0:r0 + 128, :], wvb, accum_op=ALU.add),
                        [tev, t_acc[(lg, ls)]])
                t_wvdma[gls % 2] = t_acc[(lg, ls)]
                gls += 1
            t_nld[lp] = tm
        t_b1 = t_rl
        t_b2user = tev
        t_ew2last = t_mm2l
    t_allacc = [t_acc[k] for k in t_acc]
    t_ldc = {"acc": None, "x1": None, "s1": None}
    t_out = None
    tails = []
    for lc in range(NLC):
        r0, r1 = lc * 128, (lc + 1) * 128
        tla = Y(lambda e2, r0=r0, r1=r1: e2.dma_start(acc_t, scr["acc"][r0:r1, :]),
                t_allacc if lc == 0 else [t_ldc["acc"]])
        tlx = Y(lambda e2, r0=r0, r1=r1: e2.dma_start(x1_t, scr["x1"][r0:r1, :]), [t_ldc["x1"]])
        tls = Y(lambda e2, r0=r0, r1=r1: e2.dma_start(s1_t, scr["s1"][r0:r1, :]), [t_ldc["s1"]])
        tg1 = S(lambda e2: e2.activation(acc_t, acc_t, AF.Silu), [tla, t_out])
        tm1 = V(lambda e2: e2.tensor_tensor(s1_t, s1_t, acc_t, ALU.mult), [tg1, tls])
        tm2 = V(lambda e2: e2.tensor_tensor(x1_t, x1_t, s1_t, ALU.add), [tlx, t_out])
        t_ldc["acc"] = tm1; t_ldc["s1"] = tm1; t_ldc["x1"] = tm2
        t_out = Y(lambda e2, r0=r0, r1=r1: e2.dma_start(out_d[r0:r1, :], x1_t), [tm2])
        tails = [t_out]
    return tails


INPUT_SPECS = {
    "x": (L, D), "L0_attn_W_x": (1024, 96), "L0_attn_W_dt": (64, 1024),
    "L0_attn_b_dt": (1024,), "L0_attn_A_log": (1024, 16), "L0_attn_D": (1024,),
    "L0_attn_W_out": (1024, 1024), "L0_attn_b_out": (1024,),
    "L0_ffn_weight": (1024, 1024), "L0_ffn_bias": (1024,),
    "L1_attn_w_qkv": (1024, 3072), "L1_attn_w_gate": (1024, 1024),
    "L1_attn_w_out": (1024, 1024), "L1_ffn_norm_weight": (1024,),
    "L1_ffn_router_w": (8, 1024), "L1_ffn_sw1": (1024, 4096), "L1_ffn_sb1": (4096,),
    "L1_ffn_sw2": (4096, 1024), "L1_ffn_sb2": (1024,),
    "L1_ffn_ew1": (8, 1024, 4096), "L1_ffn_eb1": (8, 4096),
    "L1_ffn_ew2": (8, 4096, 1024), "L1_ffn_eb2": (8, 1024),
    "c_triu": (128, 128), "c_iden": (128, 128), "c_ones_row": (1, 128),
    "c_ones_col": (128, 1), "c_indic": (128, 256),
}

SCRATCH_SPECS = {
    "xn": (L, D), "xdbl": (L, 96), "pd": (L, D), "du": (L, D), "g0": (L, D),
    "x1": (L, D), "s1": (L, D), "comb": (L, 8), "acc": (L, D),
    "xn1t_hbm": (128, NDT * L), "normt_hbm": (128, NDT * L), "carr": (16, 1024),
    "xd": (L, D), "wdq": (1024, 1024), "alogt": (128, 128),
    "gatt_hbm": (128, NDT * L), "qt_hbm": (64, L), "kt_hbm": (64, L), "qt0_hbm": (64, L),
    "vt_hbm": (64, L), "gt_hbm": (64, L), "am_hbm": (128, NLC * 128), "den": (NLC, 128),
    "num_hbm": (64, NLC * 128),
}


def build(phases=("A",), dbg=()):
    nc = bass.Bass("TRN2", target_bir_lowering=False)
    din = {}
    for name, shape in INPUT_SPECS.items():
        din[name] = nc.dram_tensor(name, list(shape), F32, kind="ExternalInput").ap()
    scr = {}
    for name, shape in SCRATCH_SPECS.items():
        kind = "ExternalOutput" if name in dbg else "Internal"
        dt = BF16 if name.endswith("_hbm") else F32
        scr[name] = nc.dram_tensor("s_" + name, list(shape), dt, kind=kind).ap()
    out_d = nc.dram_tensor("out", [L, D], F32, kind="ExternalOutput").ap()

    P = Prog(nc)
    ar = Arena(nc, 48600)
    C = Ctx(nc, P, ar)
    with nc.Block() as block:
        t0 = emit_const_loads(C, din)
        tails = emit_phaseA(C, din, scr, [t0])
        if "B" in phases:
            P.barrier(tails)
            ar.reset()
            tails = emit_phaseB(C, din, scr, [])
        if "C" in phases:
            P.barrier(tails)
            ar.reset()
            tails = emit_phaseC(C, din, scr, [])
        if "C2" in phases:
            P.barrier(tails)
            ar.reset()
            tails = emit_phaseC2(C, din, scr, [])
        if "D" in phases:
            P.barrier(tails)
            ar.reset()
            tails = emit_phaseD(C, din, scr, out_d, [])
        P.barrier(tails)
        P.finalize(block)
    return nc




def kernel(**inputs):
    from concourse.bass_utils import run_bass_kernel_spmd
    x = np.ascontiguousarray(inputs["L0_attn_x"])
    nc = build(phases=("A", "B", "C", "C2", "D"), dbg=())
    consts = host_constants()
    base = {k: np.ascontiguousarray(v) for k, v in inputs.items() if k != "L0_attn_x"}
    base.update(consts)
    in_maps = []
    for b in range(8):
        m = dict(base)
        m["x"] = np.ascontiguousarray(x[b])
        in_maps.append(m)
    res = run_bass_kernel_spmd(nc, in_maps, core_ids=list(range(8)))
    out = np.stack([np.asarray(res.results[b]["out"]) for b in range(8)], axis=0)
    return out.astype(np.float32)
